# version 07
# baseline (speedup 1.0000x reference)
import numpy as np
import ml_dtypes
import concourse.bass as bass
import concourse.mybir as mybir
from concourse.bass_utils import run_bass_kernel_spmd
from concourse.alu_op_type import AluOpType as OP
from contextlib import ExitStack

F32 = mybir.dt.float32
F32R = mybir.dt.float32r
BF16 = mybir.dt.bfloat16
U32 = mybir.dt.uint32
AF = mybir.ActivationFunctionType
AX = mybir.AxisListType
BF = ml_dtypes.bfloat16

B, T, D = 2, 8192, 1024
DI, DS, DC, E, DF = 2048, 16, 4, 8, 4096
NCORE, GRP = 8, 4
DIS = DI // GRP
NDP = DIS // 128
WIN, NW = 2048, 4
CH, NCH = 512, 4
BT, NB = 512, 4
EPS_LN = 1e-5
EPS_RMS = float(np.finfo(np.float32).eps)
ACT_EXPLN = 6
ACT_SILU = 18
BIG = 1e30


class Sem:
    def __init__(self, es, nc, name):
        self.h = es.enter_context(nc.semaphore(name))
        self.n = 0
        self.name = name


def _freeze(fn):
    import types
    if not fn.__closure__:
        return fn
    cells = []
    for c in fn.__closure__:
        try:
            cells.append(types.CellType(c.cell_contents))
        except ValueError:
            cells.append(c)
    return types.FunctionType(fn.__code__, fn.__globals__, fn.__name__,
                              fn.__defaults__, tuple(cells))


class Prog:
    def __init__(self, nc):
        self.nc = nc
        self.sched = []

    def emit(self, eng, fn, waits=(), incs=()):
        for s, v in waits:
            assert v <= s.n, f"wait {s.name}>={v} but only {s.n} scheduled"
        self.sched.append((eng, _freeze(fn), tuple(waits), tuple(incs)))
        for s, k in incs:
            s.n += k

    def dma(self, eng, sem, out, in_, waits=()):
        self.emit(eng, lambda e, o=out, i=in_: e.dma_start(out=o, in_=i),
                  waits=waits, incs=[(sem, 16)])

    def replay(self, block):
        for eng in ("sync", "tensor", "vector", "scalar", "gpsimd"):
            def body(e, eng=eng):
                hi = {}
                for en, fn, waits, incs in self.sched:
                    if en != eng:
                        continue
                    for s, v in waits:
                        if hi.get(id(s), -1) < v:
                            e.wait_ge(s.h, v)
                            hi[id(s)] = v
                    ins = fn(e)
                    for s, k in incs:
                        ins.then_inc(s.h, k)
            getattr(block, eng)(body)


_STAGES = ["mamba", "c_op", "c_h", "c_gate", "c_exp", "c_wd", "c_epi", "c_out", "full"]


def build_program(stage="full", dbg=()):
    nc = bass.Bass("TRN2", target_bir_lowering=False, debug=False)

    def din(name, shp, dt):
        return nc.dram_tensor(name, shp, dt, kind="ExternalInput").ap()

    x_fm = din("x_fm", [D, T], F32)
    xres_in = din("xres", [D, NB * BT], F32)
    xprojT = din("xprojT", [D, DIS], F32)
    dtprojT = din("dtprojT", [D, DIS], F32)
    bcprojT = din("bcprojT", [D, 2 * DS], F32)
    dtb_in = din("dtb", [128, NDP], F32)
    alog_in = din("alog", [128, NDP * DS], F32)
    convw_in = din("convw", [128, NDP * DC], F32)
    convb_in = din("convb", [128, NDP], F32)
    dvec_in = din("dvec", [128, NDP], F32)
    outprojT = din("outprojT", [DI, D], F32)
    winT = din("winT", [D, D], F32)
    innorm_in = din("innorm", [128, 8], F32)
    wrT_in = din("wrT", [128, 8 * E], F32)
    biasrep_in = din("biasrep", [128, E], F32)
    wg_t = din("wg_t", [E, 32, D, 128], F32)
    wu_t = din("wu_t", [E, 32, D, 128], F32)
    wd_t = din("wd_t", [E, 32, 128, D], F32)
    enorm_in = din("enorm", [128, E * 8], F32)
    outnorm_in = din("outnorm", [128, 8], F32)
    woutT = din("woutT", [D, D], F32)
    ident_in = din("ident", [128, 128], F32)
    onesc_in = din("onesc", [128, 1], F32)
    onesr_in = din("onesr", [1, 128], F32)
    selmat_in = din("selmat", [2 * DS, 2 * DS * 128], F32)
    idx_in = din("idx", [128, 16], U32)
    epsb_in = din("epsb", [128, 2], F32)

    out_hbm = nc.dram_tensor("out", [NB * BT, D], F32, kind="ExternalOutput").ap()
    dbg_y = (nc.dram_tensor("dbg_y", [DIS, T], F32, kind="ExternalOutput").ap()
             if "y" in dbg else None)
    dbg_dt = (nc.dram_tensor("dbg_dt", [DIS, T], F32, kind="ExternalOutput").ap()
              if "mid" in dbg else None)
    dbg_xc = (nc.dram_tensor("dbg_xc", [DIS, T], F32, kind="ExternalOutput").ap()
              if "mid" in dbg else None)
    dbg_bc = (nc.dram_tensor("dbg_bc", [2 * DS, T], F32, kind="ExternalOutput").ap()
              if "mid" in dbg else None)
    dbg_h = (nc.dram_tensor("dbg_h", [D, NB * BT], F32, kind="ExternalOutput").ap()
             if "h" in dbg else None)
    dbg_o1 = (nc.dram_tensor("dbg_o1", [D, NB * BT], F32, kind="ExternalOutput").ap()
              if "o1" in dbg else None)
    dbg_g = (nc.dram_tensor("dbg_g", [E, NB * BT], F32, kind="ExternalOutput").ap()
             if "g" in dbg else None)
    dbg_hx = (nc.dram_tensor("dbg_hx", [D, NB * BT], F32, kind="ExternalOutput").ap()
              if "hx" in dbg else None)
    dbg_rows = (nc.dram_tensor("dbg_rows", [4, D], F32, kind="ExternalOutput").ap()
                if "rows" in dbg else None)
    dbg_lg = (nc.dram_tensor("dbg_lg", [E, NB * BT], F32, kind="ExternalOutput").ap()
              if "lg" in dbg else None)

    ag_in = [nc.dram_tensor(f"ag_in{w}", [DIS, WIN], BF16) for w in range(NW)]
    ag_out = [nc.dram_tensor(f"ag_out{w}", [GRP * DIS, WIN], BF16) for w in range(NW)]
    groups = [[0, 1, 2, 3], [4, 5, 6, 7]]

    es = ExitStack()
    with es:
        sbuf = lambda name, shp, dt: es.enter_context(nc.sbuf_tensor(name, shp, dt))
        dtb_s = sbuf("dtb_s", [128, NDP], F32)
        a_s = sbuf("a_s", [128, NDP * DS], F32)
        convw_s = sbuf("convw_s", [128, NDP * DC], F32)
        convb_s = sbuf("convb_s", [128, NDP], F32)
        dvec_s = sbuf("dvec_s", [128, NDP], F32)
        innorm_s = sbuf("innorm_s", [128, 8], F32)
        wrT_s = sbuf("wrT_s", [128, 8 * E], F32R)
        biasrep_s = sbuf("biasrep_s", [128, E], F32)
        enorm_s = sbuf("enorm_s", [128, E * 8], F32)
        outnorm_s = sbuf("outnorm_s", [128, 8], F32)
        ident_s = sbuf("ident_s", [128, 128], F32R)
        onesc_s = sbuf("onesc_s", [128, 1], F32R)
        onesr_s = sbuf("onesr_s", [1, 128], F32R)
        idx_s = sbuf("idx_s", [128, 16], U32)
        epsb_s = sbuf("epsb_s", [128, 2], F32)
        nws_xi = sbuf("nws_xi", [1, DIS], F32R)
        nws_dt = sbuf("nws_dt", [1, DIS], F32R)
        nws_bc = sbuf("nws_bc", [1, 2 * DS], F32R)
        nws_win = sbuf("nws_win", [1, D], F32R)
        hcol = sbuf("hcol", [128, NDP * DS], F32)
        rows = sbuf("rows", [1, 2 * CH], F32)
        rowsr = sbuf("rowsr", [1, 2 * CH], F32R)
        gates_fm = sbuf("gates_fm", [E, BT], F32R)
        powm = sbuf("powm", [1, BT], F32)
        rstd_sb = sbuf("rstd_sb", [128, CH], F32)

        _uc = [0]

        def _psum(name, shp, dt=F32):
            _uc[0] += 1
            return nc.psum_tensor(f"{name}_{_uc[0]}", shp, dt)

        P = Prog(nc)
        TP = Sem(es, nc, "TP")
        VP = Sem(es, nc, "VP")
        SP = Sem(es, nc, "SP")
        GP = Sem(es, nc, "GP")
        SY = Sem(es, nc, "SY")
        SY2 = Sem(es, nc, "SY2")
        GD = Sem(es, nc, "GD")
        CC = Sem(es, nc, "CC")

        def act_table(set_id):
            def fn(e, set_id=set_id):
                return e.add_instruction(mybir.InstLoadActFuncSet(
                    name=e.bass.get_next_instruction_name(),
                    act_func_set_id=set_id, ins=[], outs=[]))
            P.emit("scalar", fn)

        def ptile(ap_, p=128):
            return ap_.rearrange("(a p) b -> p a b", p=p)

        for out, in_ in [
            (dtb_s[:], dtb_in), (a_s[:], alog_in), (convw_s[:], convw_in),
            (convb_s[:], convb_in), (dvec_s[:], dvec_in), (innorm_s[:], innorm_in),
            (wrT_s[:], wrT_in.bitcast(F32R)), (biasrep_s[:], biasrep_in),
            (enorm_s[:], enorm_in), (outnorm_s[:], outnorm_in),
            (ident_s[:], ident_in.bitcast(F32R)), (onesc_s[:], onesc_in.bitcast(F32R)),
            (onesr_s[:], onesr_in.bitcast(F32R)), (idx_s[:], idx_in),
            (epsb_s[:], epsb_in),
        ]:
            P.dma("sync", SY, out, in_)
        ini = SY.n

        act_table(ACT_EXPLN)
        P.emit("scalar", lambda e: e.activation(out=a_s[:], in_=a_s[:], func=AF.Exp),
               waits=[(SY, ini)], incs=[(SP, 1)])
        P.emit("vector", lambda e: e.tensor_scalar(
            out=a_s[:], in0=a_s[:], scalar1=-1.0, scalar2=None, op0=OP.mult),
            waits=[(SP, SP.n)], incs=[(VP, 1)])
        P.emit("vector", lambda e: e.memset(hcol[:], 0.0))
        P.emit("vector", lambda e: e.memset(powm[:], 0.0))
        P.emit("vector", lambda e: e.tensor_scalar(
            out=powm[:], in0=powm[:], scalar1=-0.5, scalar2=None, op0=OP.add),
            incs=[(VP, 1)])

        ab_es = ExitStack()
        absb = lambda name, shp, dt: ab_es.enter_context(nc.sbuf_tensor(name, shp, dt))
        xprojT_s = absb("xprojT_s", [128, 8, DIS], F32R)
        dtprojT_s = absb("dtprojT_s", [128, 8, DIS], F32R)
        bcprojT_s = absb("bcprojT_s", [128, 8, 2 * DS], F32R)
        selmat_s = absb("selmat_s", [2 * DS, 2 * DS * 128], F32R)
        for out, in_ in [
            (xprojT_s[:], ptile(xprojT.bitcast(F32R))),
            (dtprojT_s[:], ptile(dtprojT.bitcast(F32R))),
            (bcprojT_s[:], ptile(bcprojT.bitcast(F32R))),
            (selmat_s[:], selmat_in.bitcast(F32R)),
        ]:
            P.dma("sync", SY, out, in_)
        ini2 = SY.n
        with _psum("ps_ws", [1, DIS], F32) as ps_ws:
            for tgt, w_s, width in ((nws_xi, xprojT_s, DIS), (nws_dt, dtprojT_s, DIS),
                                    (nws_bc, bcprojT_s, 2 * DS)):
                for kp in range(8):
                    P.emit("tensor", lambda e, kp=kp, w_s=w_s, width=width: e.matmul(
                        ps_ws[:, 0:width], onesc_s[:], w_s[:, kp, 0:width],
                        start=(kp == 0), stop=(kp == 7)),
                        waits=[(SY, ini), (SY, ini2), (VP, VP.n)],
                        incs=[(TP, 1)] if kp == 7 else ())
                P.emit("vector", lambda e, tgt=tgt, width=width: e.tensor_scalar(
                    out=tgt[:, 0:width], in0=ps_ws[:, 0:width], scalar1=-1.0,
                    scalar2=None, op0=OP.mult),
                    waits=[(TP, TP.n)], incs=[(VP, 1)])

        x_ch = absb("x_ch", [128, 8, CH], F32R)
        adisc = absb("adisc", [128, 2, WIN], F32R)
        xsq = adisc[:].rearrange("p a b -> p (a b)").rearrange("p (a b) -> p a b", b=CH)
        xi_ch = absb("xi_ch", [128, NDP, CH + 3], BF16)
        dt_win = absb("dt_win", [128, NDP, WIN], BF16)
        xc_win = absb("xc_win", [128, NDP, WIN], BF16)
        dtxc = absb("dtxc", [128, WIN], F32)
        bc_win = absb("bc_win", [2 * DS, WIN], F32R)
        b_buf = absb("b_buf", [128, WIN], F32)
        h_buf = absb("h_buf", [128, WIN], F32)
        y_win = absb("y_win", [128, NDP, WIN], F32)
        dttmp = absb("dttmp", [128, CH], F32)
        cacc = [b_buf[:, 0:CH], b_buf[:, CH:2 * CH]]

        P.emit("vector", lambda e: e.memset(xi_ch[:, :, 0:3], 0.0), incs=[(VP, 1)])

        for w in range(NW):
            for ch in range(NCH):
                gc = w * NCH + ch
                col0 = ch * CH
                P.dma("sync", SY, x_ch[:],
                      ptile(x_fm[:, gc * CH:(gc + 1) * CH].bitcast(F32R)),
                      waits=[(TP, TP.n), (VP, VP.n), (SP, SP.n)])
                xw = SY.n
                with ExitStack() as pse:
                    pt = lambda n, s, d=F32: pse.enter_context(_psum(n, s, d))
                    ps_sum = pt("ps_sum", [1, CH])
                    ps_sq = pt("ps_sq", [1, CH])
                    ps_rep = pt("ps_rep", [128, CH])
                    ps_proj = pt("ps_proj", [128, NDP, CH])
                    ps_bc = pt("ps_bc", [2 * DS, CH])
                    for kp in range(8):
                        P.emit("tensor", lambda e, kp=kp: e.matmul(
                            ps_sum[:], onesc_s[:], x_ch[:, kp, :],
                            start=(kp == 0), stop=(kp == 7)), waits=[(SY, xw)])
                    P.emit("vector", lambda e: e.tensor_tensor(
                        out=xsq.rearrange("p a b -> p (a b)"),
                        in0=x_ch[:].rearrange("p a b -> p (a b)").bitcast(F32),
                        in1=x_ch[:].rearrange("p a b -> p (a b)").bitcast(F32),
                        op=OP.mult), waits=[(SY, xw)], incs=[(VP, 1)])
                    vsq = VP.n
                    for kp in range(8):
                        P.emit("tensor", lambda e, kp=kp: e.matmul(
                            ps_sq[:], onesc_s[:], xsq[:, kp, :],
                            start=(kp == 0), stop=(kp == 7)),
                            waits=[(VP, vsq)], incs=[(TP, 1)] if kp == 7 else ())
                    tst = TP.n
                    m_row, v_row = rows[:, 0:CH], rows[:, CH:2 * CH]
                    P.emit("vector", lambda e, m_row=m_row: e.tensor_scalar(
                        out=m_row, in0=ps_sum[:], scalar1=1.0 / D, scalar2=None,
                        op0=OP.mult), waits=[(TP, tst)])
                    P.emit("vector", lambda e, m_row=m_row, v_row=v_row:
                           e.scalar_tensor_tensor(out=v_row, in0=m_row, scalar=-1.0,
                                                  in1=m_row, op0=OP.mult, op1=OP.mult))
                    P.emit("vector", lambda e, v_row=v_row: e.scalar_tensor_tensor(
                        out=v_row, in0=ps_sq[:], scalar=1.0 / D, in1=v_row,
                        op0=OP.mult, op1=OP.add), incs=[(VP, 1)])
                    mr_r = rowsr[:, 0:CH]
                    P.emit("vector", lambda e, m_row=m_row, mr_r=mr_r: e.tensor_copy(
                        out=mr_r, in_=m_row), incs=[(VP, 1)])
                    vmr = VP.n
                    rv_r = rowsr[:, CH:2 * CH]
                    P.emit("scalar", lambda e, v_row=v_row: e.activation(
                        out=v_row, in_=v_row, func=AF.Ln, bias=epsb_s[0:1, 0:1]),
                        waits=[(VP, vmr)])
                    P.emit("scalar", lambda e, v_row=v_row, rv_r=rv_r: e.activation(
                        out=rv_r, in_=v_row, func=AF.Exp, scale=-0.5), incs=[(SP, 1)])
                    P.emit("tensor", lambda e, rv_r=rv_r: e.matmul(
                        ps_rep[:], onesr_s[:], rv_r, start=True, stop=True),
                        waits=[(SP, SP.n)], incs=[(TP, 1)])
                    P.emit("scalar", lambda e: e.copy(out=rstd_sb[:], in_=ps_rep[:]),
                           waits=[(TP, TP.n), (VP, VP.n)], incs=[(SP, 1)])
                    srstd = SP.n
                    for m in range(NDP):
                        for kp in range(8):
                            P.emit("tensor", lambda e, m=m, kp=kp: e.matmul(
                                ps_proj[:, m, :], xprojT_s[:, kp, m * 128:(m + 1) * 128],
                                x_ch[:, kp, :], start=(kp == 0), stop=False),
                                waits=[(SY, xw)])
                        P.emit("tensor", lambda e, m=m: e.matmul(
                            ps_proj[:, m, :], nws_xi[:, m * 128:(m + 1) * 128],
                            mr_r, start=False, stop=True),
                            waits=[(VP, vmr)], incs=[(TP, 1)] if m == NDP - 1 else ())
                    txi = TP.n
                    if gc > 0:
                        P.emit("vector", lambda e: e.tensor_copy(
                            out=xi_ch[:, :, 0:3], in_=xi_ch[:, :, CH:CH + 3]),
                            waits=[(SP, SP.n)])
                    for m in range(NDP):
                        P.emit("vector", lambda e, m=m: e.tensor_tensor(
                            out=xi_ch[:, m, 3:3 + CH],
                            in0=ps_proj[:, m, :], in1=rstd_sb[:], op=OP.mult),
                            waits=[(TP, txi), (SP, srstd)],
                            incs=[(VP, 1)] if m == NDP - 1 else ())
                    vxi = VP.n
                    for m in range(NDP):
                        for kp in range(8):
                            P.emit("tensor", lambda e, m=m, kp=kp: e.matmul(
                                ps_proj[:, m, :], dtprojT_s[:, kp, m * 128:(m + 1) * 128],
                                x_ch[:, kp, :], start=(kp == 0), stop=False),
                                waits=[(VP, vxi)])
                        P.emit("tensor", lambda e, m=m: e.matmul(
                            ps_proj[:, m, :], nws_dt[:, m * 128:(m + 1) * 128],
                            mr_r, start=False, stop=True), incs=[(TP, 1)])
                        tdt = TP.n
                        P.emit("vector", lambda e, m=m: e.tensor_tensor(
                            out=dttmp[:], in0=ps_proj[:, m, :], in1=rstd_sb[:],
                            op=OP.mult), waits=[(TP, tdt), (SP, SP.n)],
                            incs=[(VP, 1)])
                        P.emit("scalar", lambda e, m=m: e.activation(
                            out=dttmp[:], in_=dttmp[:], func=AF.Exp,
                            bias=dtb_s[:, m:m + 1]), waits=[(VP, VP.n)])
                        P.emit("scalar", lambda e, m=m, col0=col0: e.activation(
                            out=dt_win[:, m, col0:col0 + CH], in_=dttmp[:],
                            func=AF.Ln, bias=1.0), incs=[(SP, 1)])
                    sdt = SP.n
                    for kp in range(8):
                        P.emit("tensor", lambda e, kp=kp: e.matmul(
                            ps_bc[:], bcprojT_s[:, kp, :], x_ch[:, kp, :],
                            start=(kp == 0), stop=False),
                            waits=[(SY, xw), (SP, sdt)])
                    P.emit("tensor", lambda e: e.matmul(
                        ps_bc[:], nws_bc[:], mr_r, start=False, stop=True),
                        incs=[(TP, 1)])
                    P.emit("vector", lambda e, col0=col0: e.tensor_tensor(
                        out=bc_win[:, col0:col0 + CH], in0=ps_bc[:],
                        in1=rstd_sb[0:2 * DS, :], op=OP.mult),
                        waits=[(TP, TP.n)], incs=[(VP, 1)])
                    act_table(ACT_SILU)
                    for m in range(NDP):
                        P.emit("vector", lambda e, m=m: e.tensor_scalar(
                            out=cacc[0], in0=xi_ch[:, m, 0:CH],
                            scalar1=convw_s[:, m * DC:m * DC + 1], scalar2=None,
                            op0=OP.mult), waits=[(VP, vxi)])
                        for k in range(1, DC):
                            P.emit("vector", lambda e, m=m, k=k: e.scalar_tensor_tensor(
                                out=cacc[k % 2],
                                in0=xi_ch[:, m, k:k + CH],
                                scalar=convw_s[:, m * DC + k:m * DC + k + 1],
                                in1=cacc[(k - 1) % 2], op0=OP.mult, op1=OP.add),
                                waits=[(SP, SP.n)] if k == 1 else (),
                                incs=[(VP, 1)] if k == DC - 1 else ())
                        P.emit("scalar", lambda e, m=m, col0=col0: e.activation(
                            out=xc_win[:, m, col0:col0 + CH], in_=cacc[(DC - 1) % 2],
                            func=AF.Silu, bias=convb_s[:, m:m + 1]),
                            waits=[(VP, VP.n)], incs=[(SP, 1)])
                    act_table(ACT_EXPLN)
            wxc = SP.n

            act_table(ACT_EXPLN)
            with ExitStack() as pse:
                ps_B = pse.enter_context(_psum("ps_B", [128, WIN], F32))
                ps_C = pse.enter_context(_psum("ps_C", [128, WIN], F32))
                tts_snap = [VP.n, VP.n]
                for dp in range(NDP):
                    P.emit("vector", lambda e, dp=dp: e.tensor_tensor(
                        out=dtxc[:], in0=dt_win[:, dp, :], in1=xc_win[:, dp, :],
                        op=OP.mult), waits=[(SP, wxc)])
                    P.emit("vector", lambda e, dp=dp: e.tensor_scalar(
                        out=y_win[:, dp, :], in0=xc_win[:, dp, :],
                        scalar1=dvec_s[:, dp:dp + 1], scalar2=None, op0=OP.mult),
                        waits=[(GD, GD.n)], incs=[(VP, 1)])
                    for s in range(DS):
                        sidx = dp * DS + s
                        par = s % 2
                        for q in range(4):
                            P.emit("tensor", lambda e, s=s, q=q: e.matmul(
                                ps_B[:, q * CH:(q + 1) * CH],
                                selmat_s[:, s * 128:(s + 1) * 128],
                                bc_win[:, q * CH:(q + 1) * CH],
                                start=True, stop=True),
                                waits=[(VP, VP.n), (SP, SP.n)])
                        for q in range(4):
                            P.emit("tensor", lambda e, s=s, q=q: e.matmul(
                                ps_C[:, q * CH:(q + 1) * CH],
                                selmat_s[:, (DS + s) * 128:(DS + s + 1) * 128],
                                bc_win[:, q * CH:(q + 1) * CH],
                                start=True, stop=True),
                                incs=[(TP, 1)] if q == 3 else ())
                        trep = TP.n
                        P.emit("scalar", lambda e, dp=dp, sidx=sidx, par=par:
                               e.activation(out=adisc[:, par, :], in_=dt_win[:, dp, :],
                                            func=AF.Exp, scale=a_s[:, sidx:sidx + 1]),
                               waits=[(VP, tts_snap[par])], incs=[(SP, 1)])
                        sad = SP.n
                        P.emit("vector", lambda e: e.tensor_tensor(
                            out=b_buf[:], in0=dtxc[:], in1=ps_B[:], op=OP.mult),
                            waits=[(TP, trep)])
                        P.emit("vector", lambda e, par=par, sidx=sidx:
                               e.tensor_tensor_scan(
                                   out=h_buf[:], data0=adisc[:, par, :].bitcast(F32),
                                   data1=b_buf[:],
                                   initial=hcol[:, sidx:sidx + 1],
                                   op0=OP.mult, op1=OP.add), waits=[(SP, sad)],
                               incs=[(VP, 1)])
                        tts_snap[par] = VP.n
                        P.emit("vector", lambda e: e.tensor_tensor(
                            out=b_buf[:], in0=h_buf[:], in1=ps_C[:], op=OP.mult))
                        P.emit("vector", lambda e, dp=dp: e.tensor_tensor(
                            out=y_win[:, dp, :], in0=y_win[:, dp, :], in1=b_buf[:],
                            op=OP.add), incs=[(VP, 1)])
                        P.emit("vector", lambda e, sidx=sidx: e.tensor_copy(
                            out=hcol[:, sidx:sidx + 1], in_=h_buf[:, WIN - 1:WIN]))
            vy = VP.n
            if dbg_dt is not None:
                P.dma("gpsimd", GD, ptile(dbg_dt[:, w * WIN:(w + 1) * WIN], 128),
                      dt_win[:], waits=[(VP, vy)])
                P.dma("gpsimd", GD, ptile(dbg_xc[:, w * WIN:(w + 1) * WIN], 128),
                      xc_win[:], waits=[(VP, vy)])
                P.dma("gpsimd", GD, dbg_bc[:, w * WIN:(w + 1) * WIN],
                      bc_win[:].bitcast(F32), waits=[(VP, vy)])
            P.dma("gpsimd", GD, ptile(ag_in[w].ap(), 128), y_win[:].bitcast(F32),
                  waits=[(VP, vy)])
            if dbg_y is not None:
                P.dma("gpsimd", GD, ptile(dbg_y[:, w * WIN:(w + 1) * WIN], 128),
                      y_win[:])
            P.emit("gpsimd", lambda e, w=w: e.collective_compute(
                "AllGather", OP.bypass, replica_groups=groups,
                ins=[ag_in[w].ap().opt()], outs=[ag_out[w].ap().opt()]),
                waits=[(GD, GD.n)], incs=[(CC, 1)])

        if stage == "mamba":
            P.emit("gpsimd", lambda e: e.wait_ge(CC.h, CC.n))
            with nc.Block() as blk:
                P.replay(blk)
            ab_es.close()
            return nc

        ab_es.close()
        y_own = sbuf("y_own", [128, 16, BT], BF16)
        out1 = sbuf("out1", [128, 8, BT], F32R)
        h_sb = sbuf("h_sb", [128, 8, BT], F32R)
        gu_t = sbuf("gu_t", [128, 32, BT], BF16)
        gu_sb = gu_t[:]
        scrR_t = sbuf("scrR_t", [128, 8, BT], F32R)
        scrR = scrR_t[:]
        moe = sbuf("moe", [128, 8, BT], F32)
        scrF = sbuf("scrF", [128, 8, BT], F32)
        oe_bf = y_own[:, 0:8, :]
        gtmp = sbuf("gtmp", [128, BT], F32)
        wga = [sbuf(f"wga{i}", [128, 2, 8, 128], F32R) for i in range(2)]
        wua = [sbuf(f"wua{i}", [128, 8, 128], F32R) for i in range(2)]
        wda = [y_own[:, 8 + 4 * i:12 + 4 * i, :]
               .rearrange("p a b -> p (a b)")
               .rearrange("p (f d) -> p f d", d=D) for i in range(2)]
        opa = [sbuf(f"opa{i}", [128, D], BF16) for i in range(2)]
        wia = [sbuf(f"wia{i}", [128, D], F32R) for i in range(2)]
        woa = [sbuf(f"woa{i}", [128, D], F32R) for i in range(2)]
        t1_sb = moe[:, 0:2, :].rearrange("p a b -> p (a b)")
        lg_sb = sbuf("lg_sb", [E, BT], F32R)
        lb_sb = sbuf("lb_sb", [128, E], F32)
        msk_sb = sbuf("msk_sb", [128, E], F32)
        em_sb = sbuf("em_sb", [128, E], F32)
        gt_sb = sbuf("gt_sb", [128, E], F32R)
        col_sb = sbuf("col_sb", [128, 4], F32)
        gsc_sb = sbuf("gsc_sb", [128, 8], F32)
        rrow = rows[:, 0:BT]

        slv = _STAGES.index(stage)

        def _past(name):
            return slv >= _STAGES.index(name)

        for bt in range(NB if stage == "full" else 1):
            rows_view = ag_out[bt].ap().rearrange("a (s c) -> (a s) c", s=GRP)
            for gi in range(16):
                P.emit("gpsimd", lambda e, gi=gi, rows_view=rows_view:
                       e.indirect_dma_start(
                           out=y_own[:, gi, :], out_offset=None, in_=rows_view,
                           in_offset=bass.IndirectOffsetOnAxis(
                               ap=idx_s[:, gi:gi + 1], axis=0)),
                       waits=[(CC, NW), (TP, TP.n)], incs=[(GD, 16)])
            gy = GD.n
            P.emit("vector", lambda e: e.tensor_copy(out=col_sb[:, 0:1], in_=col_sb[:, 0:1]),
                   waits=[(SP, SP.n), (TP, TP.n), (SY, SY.n), (GD, GD.n), (GP, GP.n), (CC, CC.n)],
                   incs=[(VP, 1)])
            P.emit("scalar", lambda e: e.copy(out=col_sb[:, 1:2], in_=col_sb[:, 1:2]),
                   waits=[(VP, VP.n), (TP, TP.n), (GD, GD.n)], incs=[(SP, 1)])
            P.dma("sync", SY, scrF[:],
                  ptile(xres_in[:, bt * BT:(bt + 1) * BT]),
                  waits=[(VP, VP.n), (CC, NW), (GD, GD.n)])
            xr = SY.n
            with ExitStack() as pse:
                ps_op = pse.enter_context(_psum("ps_op", [128, 8, BT], F32))
                op_last = [TP.n, TP.n]
                for kp in range(16):
                    par = kp % 2
                    P.dma("gpsimd", GD, opa[par][:],
                          outprojT[kp * 128:(kp + 1) * 128, :],
                          waits=[(TP, op_last[par]), (CC, NW)])
                    wv = GD.n
                    for m in range(8):
                        P.emit("tensor", lambda e, kp=kp, m=m, par=par: e.matmul(
                            ps_op[:, m, :], opa[par][:, m * 128:(m + 1) * 128],
                            y_own[:, kp, :], start=(kp == 0), stop=(kp == 15)),
                            waits=[(GD, wv), (GD, gy), (SP, SP.n)],
                            incs=[(TP, 1)] if m == 7 else ())
                    op_last[par] = TP.n
                top = TP.n
                for m in range(8):
                    P.emit("vector", lambda e, m=m: e.tensor_tensor(
                        out=out1[:, m, :], in0=ps_op[:, m, :], in1=scrF[:, m, :],
                        op=OP.add), waits=[(TP, top), (SY, xr)],
                        incs=[(VP, 1)] if m == 7 else ())
            vo1 = VP.n
            if dbg_o1 is not None:
                P.dma("gpsimd", GD, ptile(dbg_o1[:, bt * BT:(bt + 1) * BT]),
                      out1[:].bitcast(F32), waits=[(VP, vo1)])
            if not _past("c_h"):
                continue
            with ExitStack() as pse:
                pt = lambda n, s: pse.enter_context(_psum(n, s, F32))
                ps_s1 = pt("ps_s1", [1, BT])
                ps_s2 = pt("ps_s2", [1, BT])
                ps_rep = pt("ps_rep", [128, BT])
                P.emit("vector", lambda e: e.tensor_tensor(
                    out=scrR.rearrange("p a b -> p (a b)"),
                    in0=out1[:].rearrange("p a b -> p (a b)").bitcast(F32),
                    in1=out1[:].rearrange("p a b -> p (a b)").bitcast(F32),
                    op=OP.mult), waits=[(VP, vo1)], incs=[(VP, 1)])
                vsq = VP.n
                for kp in range(8):
                    P.emit("tensor", lambda e, kp=kp: e.matmul(
                        ps_s1[:], onesc_s[:], out1[:, kp, :],
                        start=(kp == 0), stop=(kp == 7)), waits=[(VP, vo1)])
                for kp in range(8):
                    P.emit("tensor", lambda e, kp=kp: e.matmul(
                        ps_s2[:], onesc_s[:], scrR[:, kp, :],
                        start=(kp == 0), stop=(kp == 7)),
                        waits=[(VP, vsq)], incs=[(TP, 1)] if kp == 7 else ())
                tst = TP.n
                if dbg_rows is not None and bt == 0:
                    P.emit("vector", lambda e: e.tensor_copy(
                        out=rows[:, 0:BT], in_=ps_s1[:]), waits=[(TP, tst)])
                    P.emit("vector", lambda e: e.tensor_copy(
                        out=rows[:, BT:2 * BT], in_=ps_s2[:]), incs=[(VP, 1)])
                    P.dma("gpsimd", GD, dbg_rows[0:1, 0:2 * CH], rows[:],
                          waits=[(VP, VP.n)])
                m_row, v_row = rows[:, 0:BT], rows[:, BT:2 * BT]
                P.emit("vector", lambda e, m_row=m_row: e.tensor_scalar(
                    out=m_row, in0=ps_s1[:], scalar1=1.0 / D, scalar2=None,
                    op0=OP.mult), waits=[(TP, tst), (GD, GD.n)])
                P.emit("vector", lambda e, m_row=m_row, v_row=v_row:
                       e.scalar_tensor_tensor(out=v_row, in0=m_row, scalar=-1.0,
                                              in1=m_row, op0=OP.mult, op1=OP.mult))
                P.emit("vector", lambda e, v_row=v_row: e.scalar_tensor_tensor(
                    out=v_row, in0=ps_s2[:], scalar=1.0 / D, in1=v_row,
                    op0=OP.mult, op1=OP.add), incs=[(VP, 1)])
                mr_r = rowsr[:, 0:BT]
                P.emit("vector", lambda e, m_row=m_row, mr_r=mr_r: e.tensor_copy(
                    out=mr_r, in_=m_row), incs=[(VP, 1)])
                vmr = VP.n
                rv_r = rowsr[:, BT:2 * BT]
                P.emit("scalar", lambda e, v_row=v_row: e.activation(
                    out=v_row, in_=v_row, func=AF.Ln, bias=epsb_s[0:1, 0:1]), waits=[(VP, vmr)])
                P.emit("scalar", lambda e, v_row=v_row, rv_r=rv_r: e.activation(
                    out=rv_r, in_=v_row, func=AF.Exp, scale=-0.5), incs=[(SP, 1)])
                P.emit("tensor", lambda e, rv_r=rv_r: e.matmul(
                    ps_rep[:], onesr_s[:], rv_r, start=True, stop=True),
                    waits=[(SP, SP.n)], incs=[(TP, 1)])
                P.emit("scalar", lambda e: e.copy(out=rstd_sb[:], in_=ps_rep[:]),
                       waits=[(TP, TP.n), (VP, VP.n)], incs=[(SP, 1)])
                srstd = SP.n
                if dbg_rows is not None and bt == 0:
                    P.dma("gpsimd", GD, dbg_rows[1:2, 0:2 * CH],
                          rowsr[:].bitcast(F32), waits=[(SP, srstd)])
                    P.dma("gpsimd", GD, dbg_rows[2:3, 0:2 * CH], rows[:],
                          waits=[(SP, srstd)])
                    P.dma("gpsimd", GD, dbg_rows[3:4, 0:CH], rstd_sb[:, 0:CH][0:1, :],
                          waits=[(SP, srstd)])
            if bt == 0:
                with _psum("ps_wr", [1, D], F32) as ps_wr:
                    for kp in range(8):
                        par = kp % 2
                        P.dma("sync", SY, wia[par][:],
                              winT[kp * 128:(kp + 1) * 128, :].bitcast(F32R),
                              waits=[(TP, TP.n)])
                        wv = SY.n
                        for half in range(2):
                            P.emit("tensor", lambda e, kp=kp, par=par, half=half:
                                   e.matmul(ps_wr[:, half * BT:(half + 1) * BT],
                                            onesc_s[:],
                                            wia[par][:, half * BT:(half + 1) * BT],
                                            start=(kp == 0), stop=(kp == 7)),
                                   waits=[(SY, wv), (VP, vmr), (SP, srstd)],
                                   incs=[(TP, 1)] if (kp == 7 and half == 1) else ())
                    P.emit("vector", lambda e: e.tensor_scalar(
                        out=nws_win[:], in0=ps_wr[:], scalar1=-1.0, scalar2=None,
                        op0=OP.mult), waits=[(TP, TP.n)], incs=[(VP, 1)])
            with ExitStack() as pse:
                ps_h = pse.enter_context(_psum("ps_h", [128, 8, BT], F32))
                wi_last = [TP.n, TP.n]
                for kp in range(8):
                    par = kp % 2
                    P.dma("sync", SY, wia[par][:],
                          winT[kp * 128:(kp + 1) * 128, :].bitcast(F32R),
                          waits=[(TP, wi_last[par])])
                    wv = SY.n
                    for m in range(8):
                        P.emit("tensor", lambda e, kp=kp, m=m, par=par: e.matmul(
                            ps_h[:, m, :], wia[par][:, m * 128:(m + 1) * 128],
                            out1[:, kp, :], start=(kp == 0), stop=False),
                            waits=[(SY, wv), (VP, vmr), (SP, srstd)],
                            incs=[(TP, 1)] if m == 7 else ())
                    wi_last[par] = TP.n
                for m in range(8):
                    P.emit("tensor", lambda e, m=m: e.matmul(
                        ps_h[:, m, :], nws_win[:, m * 128:(m + 1) * 128], mr_r,
                        start=False, stop=True),
                        waits=[(VP, vmr)], incs=[(TP, 1)] if m == 7 else ())
                th = TP.n
                for m in range(8):
                    P.emit("vector", lambda e, m=m: e.tensor_tensor(
                        out=scrF[:, m, :], in0=ps_h[:, m, :], in1=rstd_sb[:, 0:BT],
                        op=OP.mult), waits=[(TP, th), (SP, srstd)],
                        incs=[(VP, 1)] if m == 7 else ())
                vhx = VP.n
            if dbg_hx is not None:
                P.dma("gpsimd", GD, ptile(dbg_hx[:, bt * BT:(bt + 1) * BT]),
                      scrF[:], waits=[(VP, vhx)])
            with ExitStack() as pse:
                pt = lambda n, s: pse.enter_context(_psum(n, s, F32))
                ps_s2 = pt("ps_s2b", [1, BT])
                ps_rep = pt("ps_repb", [128, BT])
                P.emit("vector", lambda e: e.tensor_tensor(
                    out=scrR.rearrange("p a b -> p (a b)"),
                    in0=scrF[:].rearrange("p a b -> p (a b)"),
                    in1=scrF[:].rearrange("p a b -> p (a b)"),
                    op=OP.mult), waits=[(VP, vhx)], incs=[(VP, 1)])
                vsq = VP.n
                for kp in range(8):
                    P.emit("tensor", lambda e, kp=kp: e.matmul(
                        ps_s2[:], onesc_s[:], scrR[:, kp, :],
                        start=(kp == 0), stop=(kp == 7)),
                        waits=[(VP, vsq)], incs=[(TP, 1)] if kp == 7 else ())
                tst = TP.n
                v_row = rows[:, 0:BT]
                P.emit("vector", lambda e, v_row=v_row: e.tensor_scalar(
                    out=v_row, in0=ps_s2[:], scalar1=1.0 / D, scalar2=None,
                    op0=OP.mult), waits=[(TP, tst)], incs=[(VP, 1)])
                rv_r = rowsr[:, BT:2 * BT]
                P.emit("scalar", lambda e, v_row=v_row: e.activation(
                    out=v_row, in_=v_row, func=AF.Ln, bias=epsb_s[0:1, 1:2]),
                    waits=[(VP, VP.n)])
                P.emit("scalar", lambda e, v_row=v_row, rv_r=rv_r: e.activation(
                    out=rv_r, in_=v_row, func=AF.Exp, scale=-0.5), incs=[(SP, 1)])
                P.emit("tensor", lambda e, rv_r=rv_r: e.matmul(
                    ps_rep[:], onesr_s[:], rv_r, start=True, stop=True),
                    waits=[(SP, SP.n)], incs=[(TP, 1)])
                P.emit("scalar", lambda e: e.copy(out=rstd_sb[:], in_=ps_rep[:]),
                       waits=[(TP, TP.n), (VP, VP.n)], incs=[(SP, 1)])
                srstd = SP.n
                for m in range(8):
                    P.emit("vector", lambda e, m=m: e.scalar_tensor_tensor(
                        out=h_sb[:, m, :], in0=scrF[:, m, :],
                        scalar=innorm_s[:, m:m + 1], in1=rstd_sb[:, 0:BT],
                        op0=OP.mult, op1=OP.mult), waits=[(SP, srstd)],
                        incs=[(VP, 1)] if m == 7 else ())
                vh = VP.n
            if dbg_h is not None:
                P.dma("gpsimd", GD, ptile(dbg_h[:, bt * BT:(bt + 1) * BT]),
                      h_sb[:].bitcast(F32), waits=[(VP, vh)])
            if not _past("c_gate"):
                continue
            with ExitStack() as pse:
                pt = lambda n, s, d=F32: pse.enter_context(_psum(n, s, d))
                ps_l = pt("ps_l", [E, BT])
                ps_lt = pt("ps_lt", [128, E], F32R)
                ps_gt = pt("ps_gt", [E, 128], F32R)
                for kp in range(8):
                    P.emit("tensor", lambda e, kp=kp: e.matmul(
                        ps_l[:], wrT_s[:, kp * E:(kp + 1) * E], h_sb[:, kp, :],
                        start=(kp == 0), stop=(kp == 7)),
                        waits=[(VP, vh)], incs=[(TP, 1)] if kp == 7 else ())
                P.emit("vector", lambda e: e.tensor_copy(
                    out=lg_sb[:], in_=ps_l[:]), waits=[(TP, TP.n)], incs=[(VP, 1)])
                vlg = VP.n
                if dbg_lg is not None:
                    P.dma("gpsimd", GD, dbg_lg[:, bt * BT:(bt + 1) * BT],
                          lg_sb[:].bitcast(F32), waits=[(VP, vlg)])
                for tt in range(4):
                    P.emit("tensor", lambda e, tt=tt: e.transpose(
                        ps_lt[:], lg_sb[:, tt * 128:(tt + 1) * 128],
                        ident_s[0:8, 0:8]),
                        waits=[(VP, VP.n)], incs=[(TP, 1)])
                    tlt = TP.n
                    ltf = ps_lt[:].bitcast(F32)
                    P.emit("vector", lambda e, ltf=ltf: e.tensor_tensor(
                        out=lb_sb[:], in0=ltf, in1=biasrep_s[:], op=OP.add),
                        waits=[(TP, tlt)])
                    P.emit("vector", lambda e: e.drain())
                    P.emit("vector", lambda e: e.reduce_max(
                        col_sb[:, 0:1], lb_sb[:], axis=AX.X))
                    P.emit("vector", lambda e: e.drain())
                    P.emit("vector", lambda e: e.tensor_scalar(
                        out=msk_sb[:], in0=lb_sb[:], scalar1=col_sb[:, 0:1],
                        scalar2=None, op0=OP.is_ge))
                    P.emit("vector", lambda e: e.tensor_copy(
                        out=gsc_sb[:], in_=msk_sb[:]))
                    P.emit("vector", lambda e: e.scalar_tensor_tensor(
                        out=em_sb[:], in0=msk_sb[:], scalar=-BIG, in1=lb_sb[:],
                        op0=OP.mult, op1=OP.add))
                    P.emit("vector", lambda e: e.tensor_copy(
                        out=gsc_sb[:], in_=em_sb[:]))
                    P.emit("vector", lambda e: e.drain())
                    P.emit("vector", lambda e: e.reduce_max(
                        col_sb[:, 1:2], em_sb[:], axis=AX.X))
                    P.emit("vector", lambda e: e.drain())
                    P.emit("vector", lambda e: e.tensor_scalar(
                        out=msk_sb[:], in0=lb_sb[:], scalar1=col_sb[:, 1:2],
                        scalar2=None, op0=OP.is_ge))
                    P.emit("vector", lambda e: e.tensor_copy(
                        out=gsc_sb[:], in_=msk_sb[:]))
                    P.emit("vector", lambda e, ltf=ltf: e.reduce_max(
                        col_sb[:, 3:4], ltf, axis=AX.X, negate=True))
                    P.emit("vector", lambda e: e.drain(), incs=[(VP, 1)])
                    P.emit("scalar", lambda e, ltf=ltf: e.activation(
                        out=em_sb[:], in_=ltf, func=AF.Exp,
                        bias=col_sb[:, 3:4]), waits=[(VP, VP.n)], incs=[(SP, 1)])
                    P.emit("vector", lambda e: e.tensor_copy(
                        out=gsc_sb[:], in_=em_sb[:]), waits=[(SP, SP.n)])
                    P.emit("vector", lambda e: e.tensor_tensor(
                        out=em_sb[:], in0=em_sb[:], in1=msk_sb[:], op=OP.mult))
                    P.emit("vector", lambda e: e.tensor_copy(
                        out=gsc_sb[:], in_=em_sb[:]))
                    P.emit("vector", lambda e: e.drain())
                    P.emit("vector", lambda e: e.reduce_sum(
                        col_sb[:, 0:1], em_sb[:], axis=AX.X))
                    P.emit("vector", lambda e: e.drain())
                    P.emit("vector", lambda e: e.reciprocal(
                        col_sb[:, 1:2], col_sb[:, 0:1]))
                    P.emit("vector", lambda e: e.drain())
                    P.emit("vector", lambda e: e.tensor_copy(
                        out=gsc_sb[:, 0:4], in_=col_sb[:]))
                    P.emit("vector", lambda e: e.tensor_scalar(
                        out=gt_sb[:], in0=em_sb[:], scalar1=col_sb[:, 1:2],
                        scalar2=None, op0=OP.mult), incs=[(VP, 1)])
                    P.emit("tensor", lambda e: e.transpose(
                        ps_gt[:], gt_sb[:], ident_s[:]),
                        waits=[(VP, VP.n)], incs=[(TP, 1)])
                    P.emit("vector", lambda e, tt=tt: e.tensor_copy(
                        out=gates_fm[:, tt * 128:(tt + 1) * 128],
                        in_=ps_gt[:]),
                        waits=[(TP, TP.n)], incs=[(VP, 1)])
                vg = VP.n
            if dbg_g is not None:
                P.dma("gpsimd", GD, dbg_g[:, bt * BT:(bt + 1) * BT],
                      gates_fm[:].bitcast(F32), waits=[(VP, vg)])
            if not _past("c_exp"):
                continue
            act_table(ACT_SILU)
            for ex in range(E if _past("c_epi") else 1):
                with ExitStack() as pse:
                    ps_g = [pse.enter_context(_psum(f"ps_g{i}", [128, BT], F32))
                            for i in range(2)]
                    ps_u = [pse.enter_context(_psum(f"ps_u{i}", [128, BT], F32))
                            for i in range(2)]
                    mm_last = [TP.n, TP.n]
                    u_last = [TP.n, TP.n]
                    gu_last = [VP.n, VP.n]
                    for fb2 in range(16):
                        par = fb2 % 2
                        P.dma("sync", SY, wga[par][:],
                              wg_t[ex, 2 * fb2:2 * fb2 + 2]
                              .rearrange("f (a p) b -> p f a b", p=128).bitcast(F32R),
                              waits=[(TP, mm_last[par])])
                        wv0 = SY.n
                        for half in range(2):
                            fb = 2 * fb2 + half
                            P.dma("scalar", SY2, wua[half][:],
                                  ptile(wu_t[ex, fb].bitcast(F32R)),
                                  waits=[(TP, u_last[half])])
                            wvu = SY2.n
                            for kp in range(8):
                                P.emit("tensor", lambda e, kp=kp, par=par, half=half:
                                       e.matmul(ps_g[par][:],
                                                wga[par][:, half, kp, :],
                                                h_sb[:, kp, :],
                                                start=(kp == 0), stop=(kp == 7)),
                                       waits=[(SY, wv0),
                                              (VP, gu_last[par])])
                            for kp in range(8):
                                P.emit("tensor", lambda e, kp=kp, par=par, half=half:
                                       e.matmul(ps_u[par][:],
                                                wua[half][:, kp, :],
                                                h_sb[:, kp, :],
                                                start=(kp == 0), stop=(kp == 7)),
                                       waits=[(SY2, wvu)],
                                       incs=[(TP, 1)] if kp == 7 else ())
                            u_last[half] = TP.n
                            tgu = TP.n
                            P.emit("scalar", lambda e, par=par: e.activation(
                                out=gtmp[:], in_=ps_g[par][:], func=AF.Silu),
                                waits=[(TP, tgu), (VP, gu_last[1 - par])],
                                incs=[(SP, 1)])
                            P.emit("vector", lambda e, fb=fb, par=par: e.tensor_tensor(
                                out=gu_sb[:, fb, :], in0=ps_u[par][:], in1=gtmp[:],
                                op=OP.mult), waits=[(SP, SP.n)], incs=[(VP, 1)])
                            gu_last[par] = VP.n
                        mm_last[par] = TP.n
                vgu = VP.n
                if not _past("c_wd"):
                    continue
                with ExitStack() as pse:
                    ps_oe = pse.enter_context(_psum("ps_oe", [128, 8, BT], F32))
                    wd_last = [TP.n, TP.n]
                    for kf2 in range(16):
                        par = kf2 % 2
                        P.dma("gpsimd", GD, wda[par],
                              wd_t[ex, 2 * kf2:2 * kf2 + 2]
                              .rearrange("f p d -> p f d"),
                              waits=[(TP, wd_last[par])])
                        wv = GD.n
                        for half in range(2):
                            kfb = 2 * kf2 + half
                            for m in range(8):
                                P.emit("tensor", lambda e, kfb=kfb, m=m, par=par,
                                       half=half: e.matmul(
                                           ps_oe[:, m, :],
                                           wda[par][:, half, m * 128:(m + 1) * 128],
                                           gu_sb[:, kfb, :],
                                           start=(kfb == 0), stop=(kfb == 31)),
                                       waits=[(GD, wv), (VP, vgu)],
                                       incs=[(TP, 1)] if m == 7 else ())
                        wd_last[par] = TP.n
                    toe = TP.n
                    for m in range(8):
                        P.emit("vector", lambda e, m=m: e.tensor_copy(
                            out=oe_bf[:, m, :], in_=ps_oe[:, m, :]),
                            waits=[(TP, toe)], incs=[(VP, 1)] if m == 7 else ())
                    voe = VP.n
                if not _past("c_epi"):
                    continue
                with ExitStack() as pse:
                    pt = lambda n, s: pse.enter_context(_psum(n, s, F32))
                    ps_s2 = pt("ps_s2e", [1, BT])
                    ps_gr = pt("ps_gr", [128, BT])
                    ps_gx = pt("ps_gx", [1, BT])
                    P.emit("vector", lambda e: e.tensor_tensor(
                        out=scrR.rearrange("p a b -> p (a b)"),
                        in0=oe_bf.rearrange("p a b -> p (a b)"),
                        in1=oe_bf.rearrange("p a b -> p (a b)"),
                        op=OP.mult), waits=[(VP, voe)], incs=[(VP, 1)])
                    vsq = VP.n
                    for kp in range(8):
                        P.emit("tensor", lambda e, kp=kp: e.matmul(
                            ps_s2[:], onesc_s[:], scrR[:, kp, :],
                            start=(kp == 0), stop=(kp == 7)),
                            waits=[(VP, vsq)], incs=[(TP, 1)] if kp == 7 else ())
                    tst = TP.n
                    P.emit("vector", lambda e: e.tensor_scalar(
                        out=rrow, in0=ps_s2[:], scalar1=1.0 / D, scalar2=EPS_RMS,
                        op0=OP.mult, op1=OP.add), waits=[(TP, tst)], incs=[(VP, 1)])
                    P.emit("gpsimd", lambda e: e.tensor_tensor(
                        out=rrow, in0=rrow, in1=powm[:], op=OP.pow),
                        waits=[(VP, VP.n)], incs=[(GP, 1)])
                    P.emit("tensor", lambda e, ex=ex: e.matmul(
                        ps_gx[:], ident_s[0:8, ex:ex + 1], gates_fm[0:8, :],
                        start=True, stop=True),
                        waits=[(VP, vg)], incs=[(TP, 1)])
                    P.emit("vector", lambda e, ex=ex: e.tensor_tensor(
                        out=rowsr[:, 0:BT], in0=rrow, in1=ps_gx[:],
                        op=OP.mult), waits=[(GP, GP.n), (TP, TP.n)], incs=[(VP, 1)])
                    P.emit("tensor", lambda e: e.matmul(
                        ps_gr[:], onesr_s[:], rowsr[:, 0:BT], start=True, stop=True),
                        waits=[(VP, VP.n)], incs=[(TP, 1)])
                    tgr = TP.n
                    for m in range(8):
                        P.emit("vector", lambda e, ex=ex, m=m: e.scalar_tensor_tensor(
                            out=scrF[:, m, :], in0=oe_bf[:, m, :],
                            scalar=enorm_s[:, ex * 8 + m:ex * 8 + m + 1],
                            in1=ps_gr[:], op0=OP.mult, op1=OP.mult),
                            waits=[(TP, tgr)])
                        if ex == 0:
                            P.emit("vector", lambda e, m=m: e.tensor_copy(
                                out=moe[:, m, :], in_=scrF[:, m, :]),
                                incs=[(VP, 1)] if m == 7 else ())
                        else:
                            P.emit("vector", lambda e, m=m: e.tensor_tensor(
                                out=moe[:, m, :], in0=moe[:, m, :], in1=scrF[:, m, :],
                                op=OP.add), incs=[(VP, 1)] if m == 7 else ())
            vmoe = VP.n
            if not _past("c_out"):
                continue
            act_table(ACT_EXPLN)
            with ExitStack() as pse:
                pt = lambda n, s: pse.enter_context(_psum(n, s, F32))
                ps_s2 = pt("ps_s2o", [1, BT])
                ps_rep = pt("ps_repo", [128, BT])
                P.emit("vector", lambda e: e.tensor_tensor(
                    out=scrR.rearrange("p a b -> p (a b)"),
                    in0=moe[:].rearrange("p a b -> p (a b)"),
                    in1=moe[:].rearrange("p a b -> p (a b)"),
                    op=OP.mult), waits=[(VP, vmoe)], incs=[(VP, 1)])
                vsq = VP.n
                for kp in range(8):
                    P.emit("tensor", lambda e, kp=kp: e.matmul(
                        ps_s2[:], onesc_s[:], scrR[:, kp, :],
                        start=(kp == 0), stop=(kp == 7)),
                        waits=[(VP, vsq)], incs=[(TP, 1)] if kp == 7 else ())
                tst = TP.n
                v_row = rows[:, 0:BT]
                P.emit("vector", lambda e, v_row=v_row: e.tensor_scalar(
                    out=v_row, in0=ps_s2[:], scalar1=1.0 / D, scalar2=None,
                    op0=OP.mult), waits=[(TP, tst)], incs=[(VP, 1)])
                rv_r = rowsr[:, BT:2 * BT]
                P.emit("scalar", lambda e, v_row=v_row: e.activation(
                    out=v_row, in_=v_row, func=AF.Ln, bias=epsb_s[0:1, 1:2]),
                    waits=[(VP, VP.n)])
                P.emit("scalar", lambda e, v_row=v_row, rv_r=rv_r: e.activation(
                    out=rv_r, in_=v_row, func=AF.Exp, scale=-0.5), incs=[(SP, 1)])
                P.emit("tensor", lambda e, rv_r=rv_r: e.matmul(
                    ps_rep[:], onesr_s[:], rv_r, start=True, stop=True),
                    waits=[(SP, SP.n)], incs=[(TP, 1)])
                P.emit("scalar", lambda e: e.copy(out=rstd_sb[:], in_=ps_rep[:]),
                       waits=[(TP, TP.n), (VP, VP.n)], incs=[(SP, 1)])
                srstd = SP.n
                for m in range(8):
                    P.emit("vector", lambda e, m=m: e.scalar_tensor_tensor(
                        out=scrR[:, m, :], in0=moe[:, m, :],
                        scalar=outnorm_s[:, m:m + 1], in1=rstd_sb[:, 0:BT],
                        op0=OP.mult, op1=OP.mult), waits=[(SP, srstd)],
                        incs=[(VP, 1)] if m == 7 else ())
                vnrm = VP.n
            with ExitStack() as pse:
                ps_f = pse.enter_context(_psum("ps_f", [128, 4, 2, BT], F32))
                wo_last = [TP.n, TP.n]
                for kp in range(8):
                    par = kp % 2
                    P.dma("sync", SY, woa[par][:],
                          woutT[kp * 128:(kp + 1) * 128, :].bitcast(F32R),
                          waits=[(TP, wo_last[par])])
                    wv = SY.n
                    for tt in range(4):
                        for no2 in range(2):
                            P.emit("tensor", lambda e, kp=kp, tt=tt, no2=no2, par=par:
                                   e.matmul(ps_f[:, tt, no2, :],
                                            scrR[:, kp, tt * 128:(tt + 1) * 128],
                                            woa[par][:, no2 * BT:(no2 + 1) * BT],
                                            start=(kp == 0), stop=(kp == 7)),
                                   waits=[(SY, wv), (VP, vnrm)],
                                   incs=[(TP, 1)] if (tt == 3 and no2 == 1)
                                   else ())
                    wo_last[par] = TP.n
                tf = TP.n
                for tt in range(4):
                    for no2 in range(2):
                        P.emit("vector", lambda e, tt=tt, no2=no2: e.tensor_copy(
                            out=scrF[:, 2 * tt + no2, :], in_=ps_f[:, tt, no2, :]),
                            waits=[(TP, tf)],
                            incs=[(VP, 1)] if (tt == 3 and no2 == 1) else ())
                vf = VP.n
            with ExitStack() as pse:
                ps_t1 = pse.enter_context(_psum("ps_t1", [128, 8, 128], F32R))
                for tt in range(4):
                    for m in range(8):
                        P.emit("tensor", lambda e, tt=tt, m=m: e.transpose(
                            ps_t1[:, m, :], out1[:, m, tt * 128:(tt + 1) * 128],
                            ident_s[:]),
                            waits=[(VP, VP.n), (SP, SP.n)],
                            incs=[(TP, 1)] if m == 7 else ())
                    tt1 = TP.n
                    P.emit("scalar", lambda e: e.copy(
                        out=t1_sb, in_=ps_t1[:].rearrange("p a b -> p (a b)")
                        .bitcast(F32)), waits=[(TP, tt1)], incs=[(SP, 1)])
                    st1 = SP.n
                    for no2 in range(2):
                        P.emit("vector", lambda e, tt=tt, no2=no2: e.tensor_tensor(
                            out=scrF[:, 2 * tt + no2, :],
                            in0=scrF[:, 2 * tt + no2, :],
                            in1=t1_sb[:, no2 * BT:(no2 + 1) * BT], op=OP.add),
                            waits=[(SP, st1)],
                            incs=[(VP, 1)] if no2 == 1 else ())
                    vfin = VP.n
                    P.dma("gpsimd", GD,
                          out_hbm[bt * BT + tt * 128: bt * BT + (tt + 1) * 128, :],
                          scrF[:, 2 * tt:2 * tt + 2, :]
                          .rearrange("p a b -> p (a b)"), waits=[(VP, vfin)])
        P.emit("gpsimd", lambda e: e.wait_ge(GD.h, GD.n))
        with nc.Block() as blk:
            P.replay(blk)
        return nc


def _prep_core(inp, c):
    b, r = c // 4, c % 4
    sl = slice(r * DIS, (r + 1) * DIS)
    f32 = lambda a: np.ascontiguousarray(a, dtype=np.float32)
    x = inp["L0_attn_x"][b]
    x_fm = f32(x.T)
    own_cols = np.concatenate([np.arange(512 * (4 * w + r), 512 * (4 * w + r) + BT)
                               for w in range(NW)])
    xres = f32(x_fm[:, own_cols])
    pt = lambda a, n: f32(a).reshape(NDP, 128, n).transpose(1, 0, 2).reshape(128, NDP * n)
    d = {
        "x_fm": x_fm,
        "xres": xres,
        "xprojT": f32(inp["L0_attn_x_proj_w"][sl].T),
        "dtprojT": f32(inp["L0_attn_dt_proj_w"][sl].T),
        "bcprojT": f32(np.concatenate([inp["L0_attn_B_proj_w"],
                                       inp["L0_attn_C_proj_w"]], 0).T),
        "dtb": pt(inp["L0_attn_dt_proj_b"][sl], 1),
        "alog": pt(inp["L0_attn_A_log"][sl], DS),
        "convw": pt(inp["L0_attn_conv_w"][sl, 0, :], DC),
        "convb": pt(inp["L0_attn_conv_b"][sl], 1),
        "dvec": pt(inp["L0_attn_D"][sl], 1),
        "outprojT": f32(inp["L0_attn_out_proj_w"].T),
        "winT": f32(inp["L0_ffn_W_in"].T),
        "innorm": f32(inp["L0_ffn_in_norm_w"]).reshape(8, 128).T.copy(),
        "wrT": f32(inp["L0_ffn_Wr"].T).reshape(8, 128, E).transpose(1, 0, 2)
                .reshape(128, 8 * E).copy(),
        "biasrep": np.broadcast_to(f32(inp["L0_ffn_expert_bias"]), (128, E)).copy(),
        "wg_t": f32(inp["L0_ffn_Wg"]).transpose(0, 2, 1)
                .reshape(E, D, 32, 128).transpose(0, 2, 1, 3).copy(),
        "wu_t": f32(inp["L0_ffn_Wu"]).transpose(0, 2, 1)
                .reshape(E, D, 32, 128).transpose(0, 2, 1, 3).copy(),
        "wd_t": f32(inp["L0_ffn_Wd"]).transpose(0, 2, 1)
                .reshape(E, 32, 128, D).copy(),
        "enorm": f32(inp["L0_ffn_enorm_w"]).reshape(E, 8, 128)
                .transpose(2, 0, 1).reshape(128, E * 8).copy(),
        "outnorm": f32(inp["L0_ffn_out_norm_w"]).reshape(8, 128).T.copy(),
        "woutT": f32(inp["L0_ffn_W_out"].T),
        "ident": np.eye(128, dtype=np.float32),
        "onesc": np.ones((128, 1), np.float32),
        "onesr": np.ones((1, 128), np.float32),
        "idx": (np.arange(128)[:, None] * GRP + np.arange(16)[None, :] * 128 * GRP
                + r).astype(np.uint32),
        "selmat": np.repeat(np.eye(2 * DS, dtype=np.float32), 128, axis=1),
        "epsb": np.stack([np.full(128, EPS_LN, np.float32),
                          np.full(128, EPS_RMS, np.float32)], 1),
    }
    return d


def kernel(**inputs):
    in_maps = [_prep_core(inputs, c) for c in range(NCORE)]
    nc = build_program("full")
    res = run_bass_kernel_spmd(nc, in_maps, core_ids=list(range(NCORE)))
    out = np.zeros((B, T, D), dtype=np.float32)
    for c in range(NCORE):
        b, r = c // 4, c % 4
        oc = res.results[c]["out"]
        for w in range(NW):
            t0 = 512 * (4 * w + r)
            out[b, t0:t0 + BT, :] = oc[w * BT:(w + 1) * BT, :]
    return out


# version 08
# speedup vs baseline: 1.2217x; 1.2217x over previous
import numpy as np
import ml_dtypes
import concourse.bass as bass
import concourse.mybir as mybir
from concourse.bass_utils import run_bass_kernel_spmd
from concourse.alu_op_type import AluOpType as OP
from contextlib import ExitStack

F32 = mybir.dt.float32
F32R = mybir.dt.float32r
BF16 = mybir.dt.bfloat16
U32 = mybir.dt.uint32
AF = mybir.ActivationFunctionType
AX = mybir.AxisListType
BF = ml_dtypes.bfloat16

B, T, D = 2, 8192, 1024
DI, DS, DC, E, DF = 2048, 16, 4, 8, 4096
NCORE, GRP = 8, 4
DIS = DI // GRP
NDP = DIS // 128
WIN, NW = 2048, 4
CH, NCH = 512, 4
BT, NB = 512, 4
EPS_LN = 1e-5
EPS_RMS = float(np.finfo(np.float32).eps)
ACT_EXPLN = 6
ACT_SILU = 18
BIG = 1e30


class Sem:
    def __init__(self, es, nc, name):
        self.h = es.enter_context(nc.semaphore(name))
        self.n = 0
        self.name = name


def _freeze(fn):
    import types
    if not fn.__closure__:
        return fn
    cells = []
    for c in fn.__closure__:
        try:
            cells.append(types.CellType(c.cell_contents))
        except ValueError:
            cells.append(c)
    return types.FunctionType(fn.__code__, fn.__globals__, fn.__name__,
                              fn.__defaults__, tuple(cells))


class Prog:
    def __init__(self, nc):
        self.nc = nc
        self.sched = []

    def emit(self, eng, fn, waits=(), incs=()):
        for s, v in waits:
            assert v <= s.n, f"wait {s.name}>={v} but only {s.n} scheduled"
        self.sched.append((eng, _freeze(fn), tuple(waits), tuple(incs)))
        for s, k in incs:
            s.n += k

    def dma(self, eng, sem, out, in_, waits=()):
        self.emit(eng, lambda e, o=out, i=in_: e.dma_start(out=o, in_=i),
                  waits=waits, incs=[(sem, 16)])

    def replay(self, block):
        for eng in ("sync", "tensor", "vector", "scalar", "gpsimd"):
            def body(e, eng=eng):
                hi = {}
                for en, fn, waits, incs in self.sched:
                    if en != eng:
                        continue
                    for s, v in waits:
                        if hi.get(id(s), -1) < v:
                            e.wait_ge(s.h, v)
                            hi[id(s)] = v
                    ins = fn(e)
                    for s, k in incs:
                        ins.then_inc(s.h, k)
            getattr(block, eng)(body)


_STAGES = ["mamba", "c_op", "c_h", "c_gate", "c_exp", "c_wd", "c_epi", "c_out", "full"]


def build_program(stage="full", dbg=()):
    nc = bass.Bass("TRN2", target_bir_lowering=False, debug=False)

    def din(name, shp, dt):
        return nc.dram_tensor(name, shp, dt, kind="ExternalInput").ap()

    x_fm = din("x_fm", [D, T], F32)
    xres_in = din("xres", [D, NB * BT], F32)
    xprojT = din("xprojT", [D, DIS], F32)
    dtprojT = din("dtprojT", [D, DIS], F32)
    bcprojT = din("bcprojT", [D, 2 * DS], F32)
    dtb_in = din("dtb", [128, NDP], F32)
    alog_in = din("alog", [128, NDP * DS], F32)
    convw_in = din("convw", [128, NDP * DC], F32)
    convb_in = din("convb", [128, NDP], F32)
    dvec_in = din("dvec", [128, NDP], F32)
    outprojT = din("outprojT", [DI, D], F32)
    winT = din("winT", [D, D], F32)
    innorm_in = din("innorm", [128, 8], F32)
    wrT_in = din("wrT", [128, 8 * E], F32)
    biasrep_in = din("biasrep", [128, E], F32)
    wg_t = din("wg_t", [E, 32, D, 128], F32)
    wu_t = din("wu_t", [E, 32, D, 128], F32)
    wd_t = din("wd_t", [E, 32, 128, D], F32)
    enorm_in = din("enorm", [128, E * 8], F32)
    outnorm_in = din("outnorm", [128, 8], F32)
    woutT = din("woutT", [D, D], F32)
    ident_in = din("ident", [128, 128], F32)
    onesc_in = din("onesc", [128, 1], F32)
    onesr_in = din("onesr", [1, 128], F32)
    selmat_in = din("selmat", [2 * DS, 2 * DS * 128], F32)
    idx_in = din("idx", [128, 16], U32)
    epsb_in = din("epsb", [128, 2], F32)

    out_hbm = nc.dram_tensor("out", [NB * BT, D], F32, kind="ExternalOutput").ap()
    dbg_y = (nc.dram_tensor("dbg_y", [DIS, T], F32, kind="ExternalOutput").ap()
             if "y" in dbg else None)
    dbg_dt = (nc.dram_tensor("dbg_dt", [DIS, T], F32, kind="ExternalOutput").ap()
              if "mid" in dbg else None)
    dbg_xc = (nc.dram_tensor("dbg_xc", [DIS, T], F32, kind="ExternalOutput").ap()
              if "mid" in dbg else None)
    dbg_bc = (nc.dram_tensor("dbg_bc", [2 * DS, T], F32, kind="ExternalOutput").ap()
              if "mid" in dbg else None)
    dbg_h = (nc.dram_tensor("dbg_h", [D, NB * BT], F32, kind="ExternalOutput").ap()
             if "h" in dbg else None)
    dbg_o1 = (nc.dram_tensor("dbg_o1", [D, NB * BT], F32, kind="ExternalOutput").ap()
              if "o1" in dbg else None)
    dbg_g = (nc.dram_tensor("dbg_g", [E, NB * BT], F32, kind="ExternalOutput").ap()
             if "g" in dbg else None)
    dbg_hx = (nc.dram_tensor("dbg_hx", [D, NB * BT], F32, kind="ExternalOutput").ap()
              if "hx" in dbg else None)
    dbg_rows = (nc.dram_tensor("dbg_rows", [4, D], F32, kind="ExternalOutput").ap()
                if "rows" in dbg else None)
    dbg_lg = (nc.dram_tensor("dbg_lg", [E, NB * BT], F32, kind="ExternalOutput").ap()
              if "lg" in dbg else None)

    ag_in = [nc.dram_tensor(f"ag_in{w}", [DIS, WIN], BF16) for w in range(NW)]
    ag_out = [nc.dram_tensor(f"ag_out{w}", [GRP * DIS, WIN], BF16) for w in range(NW)]
    groups = [[0, 1, 2, 3], [4, 5, 6, 7]]

    es = ExitStack()
    with es:
        sbuf = lambda name, shp, dt: es.enter_context(nc.sbuf_tensor(name, shp, dt))
        dtb_s = sbuf("dtb_s", [128, NDP], F32)
        a_s = sbuf("a_s", [128, NDP * DS], F32)
        convw_s = sbuf("convw_s", [128, NDP * DC], F32)
        convb_s = sbuf("convb_s", [128, NDP], F32)
        dvec_s = sbuf("dvec_s", [128, NDP], F32)
        innorm_s = sbuf("innorm_s", [128, 8], F32)
        wrT_s = sbuf("wrT_s", [128, 8 * E], F32R)
        biasrep_s = sbuf("biasrep_s", [128, E], F32)
        enorm_s = sbuf("enorm_s", [128, E * 8], F32)
        outnorm_s = sbuf("outnorm_s", [128, 8], F32)
        ident_s = sbuf("ident_s", [128, 128], F32R)
        onesc_s = sbuf("onesc_s", [128, 1], F32R)
        onesr_s = sbuf("onesr_s", [1, 128], F32R)
        idx_s = sbuf("idx_s", [128, 16], U32)
        epsb_s = sbuf("epsb_s", [128, 2], F32)
        nws_xi = sbuf("nws_xi", [1, DIS], F32R)
        nws_dt = sbuf("nws_dt", [1, DIS], F32R)
        nws_bc = sbuf("nws_bc", [1, 2 * DS], F32R)
        nws_win = sbuf("nws_win", [1, D], F32R)
        hcol = sbuf("hcol", [128, NDP * DS], F32)
        rows = sbuf("rows", [1, 2 * CH], F32)
        rowsr = sbuf("rowsr", [1, 2 * CH], F32R)
        gates_fm = sbuf("gates_fm", [E, BT], F32R)
        powm = sbuf("powm", [1, BT], F32)
        rstd_sb = sbuf("rstd_sb", [128, CH], F32)

        _uc = [0]

        def _psum(name, shp, dt=F32):
            _uc[0] += 1
            return nc.psum_tensor(f"{name}_{_uc[0]}", shp, dt)

        P = Prog(nc)
        TP = Sem(es, nc, "TP")
        VP = Sem(es, nc, "VP")
        SP = Sem(es, nc, "SP")
        GP = Sem(es, nc, "GP")
        SY = Sem(es, nc, "SY")
        SY2 = Sem(es, nc, "SY2")
        GD = Sem(es, nc, "GD")
        CC = Sem(es, nc, "CC")

        def act_table(set_id):
            def fn(e, set_id=set_id):
                return e.add_instruction(mybir.InstLoadActFuncSet(
                    name=e.bass.get_next_instruction_name(),
                    act_func_set_id=set_id, ins=[], outs=[]))
            P.emit("scalar", fn)

        def ptile(ap_, p=128):
            return ap_.rearrange("(a p) b -> p a b", p=p)

        for out, in_ in [
            (dtb_s[:], dtb_in), (a_s[:], alog_in), (convw_s[:], convw_in),
            (convb_s[:], convb_in), (dvec_s[:], dvec_in), (innorm_s[:], innorm_in),
            (wrT_s[:], wrT_in.bitcast(F32R)), (biasrep_s[:], biasrep_in),
            (enorm_s[:], enorm_in), (outnorm_s[:], outnorm_in),
            (ident_s[:], ident_in.bitcast(F32R)), (onesc_s[:], onesc_in.bitcast(F32R)),
            (onesr_s[:], onesr_in.bitcast(F32R)), (idx_s[:], idx_in),
            (epsb_s[:], epsb_in),
        ]:
            P.dma("sync", SY, out, in_)
        ini = SY.n

        act_table(ACT_EXPLN)
        P.emit("scalar", lambda e: e.activation(out=a_s[:], in_=a_s[:], func=AF.Exp),
               waits=[(SY, ini)], incs=[(SP, 1)])
        P.emit("vector", lambda e: e.tensor_scalar(
            out=a_s[:], in0=a_s[:], scalar1=-1.0, scalar2=None, op0=OP.mult),
            waits=[(SP, SP.n)], incs=[(VP, 1)])
        P.emit("vector", lambda e: e.memset(hcol[:], 0.0))
        P.emit("vector", lambda e: e.memset(powm[:], 0.0))
        P.emit("vector", lambda e: e.tensor_scalar(
            out=powm[:], in0=powm[:], scalar1=-0.5, scalar2=None, op0=OP.add),
            incs=[(VP, 1)])

        ab_es = ExitStack()
        absb = lambda name, shp, dt: ab_es.enter_context(nc.sbuf_tensor(name, shp, dt))
        xprojT_s = absb("xprojT_s", [128, 8, DIS], F32R)
        dtprojT_s = absb("dtprojT_s", [128, 8, DIS], F32R)
        bcprojT_s = absb("bcprojT_s", [128, 8, 2 * DS], F32R)
        selmat_s = absb("selmat_s", [2 * DS, 2 * DS * 128], F32R)
        for out, in_ in [
            (xprojT_s[:], ptile(xprojT.bitcast(F32R))),
            (dtprojT_s[:], ptile(dtprojT.bitcast(F32R))),
            (bcprojT_s[:], ptile(bcprojT.bitcast(F32R))),
            (selmat_s[:], selmat_in.bitcast(F32R)),
        ]:
            P.dma("sync", SY, out, in_)
        ini2 = SY.n
        with _psum("ps_ws", [1, DIS], F32) as ps_ws:
            for tgt, w_s, width in ((nws_xi, xprojT_s, DIS), (nws_dt, dtprojT_s, DIS),
                                    (nws_bc, bcprojT_s, 2 * DS)):
                for kp in range(8):
                    P.emit("tensor", lambda e, kp=kp, w_s=w_s, width=width: e.matmul(
                        ps_ws[:, 0:width], onesc_s[:], w_s[:, kp, 0:width],
                        start=(kp == 0), stop=(kp == 7)),
                        waits=[(SY, ini), (SY, ini2), (VP, VP.n)],
                        incs=[(TP, 1)] if kp == 7 else ())
                P.emit("vector", lambda e, tgt=tgt, width=width: e.tensor_scalar(
                    out=tgt[:, 0:width], in0=ps_ws[:, 0:width], scalar1=-1.0,
                    scalar2=None, op0=OP.mult),
                    waits=[(TP, TP.n)], incs=[(VP, 1)])

        x_ch = absb("x_ch", [128, 8, CH], F32R)
        adisc = absb("adisc", [128, 2, WIN], F32R)
        xsq = adisc[:].rearrange("p a b -> p (a b)").rearrange("p (a b) -> p a b", b=CH)
        xi_ch = absb("xi_ch", [128, NDP, CH + 3], BF16)
        dt_win = absb("dt_win", [128, NDP, WIN], BF16)
        xc_win = absb("xc_win", [128, NDP, WIN], BF16)
        dtxc = absb("dtxc", [128, WIN], F32)
        bc_win = absb("bc_win", [2 * DS, WIN], F32R)
        b_buf = absb("b_buf", [128, WIN], F32)
        h_buf = absb("h_buf", [128, WIN], F32)
        y_win = absb("y_win", [128, NDP, WIN], F32)
        dttmp = absb("dttmp", [128, CH], F32)
        cacc = [b_buf[:, 0:CH], b_buf[:, CH:2 * CH]]

        P.emit("vector", lambda e: e.memset(xi_ch[:, :, 0:3], 0.0), incs=[(VP, 1)])

        for w in range(NW):
            for ch in range(NCH):
                gc = w * NCH + ch
                col0 = ch * CH
                P.dma("sync", SY, x_ch[:],
                      ptile(x_fm[:, gc * CH:(gc + 1) * CH].bitcast(F32R)),
                      waits=[(TP, TP.n), (VP, VP.n), (SP, SP.n)])
                xw = SY.n
                with ExitStack() as pse:
                    pt = lambda n, s, d=F32: pse.enter_context(_psum(n, s, d))
                    ps_sum = pt("ps_sum", [1, CH])
                    ps_sq = pt("ps_sq", [1, CH])
                    ps_rep = pt("ps_rep", [128, CH])
                    ps_proj = pt("ps_proj", [128, NDP, CH])
                    ps_bc = pt("ps_bc", [2 * DS, CH])
                    for kp in range(8):
                        P.emit("tensor", lambda e, kp=kp: e.matmul(
                            ps_sum[:], onesc_s[:], x_ch[:, kp, :],
                            start=(kp == 0), stop=(kp == 7)), waits=[(SY, xw)])
                    P.emit("vector", lambda e: e.tensor_tensor(
                        out=xsq.rearrange("p a b -> p (a b)"),
                        in0=x_ch[:].rearrange("p a b -> p (a b)").bitcast(F32),
                        in1=x_ch[:].rearrange("p a b -> p (a b)").bitcast(F32),
                        op=OP.mult), waits=[(SY, xw)], incs=[(VP, 1)])
                    vsq = VP.n
                    for kp in range(8):
                        P.emit("tensor", lambda e, kp=kp: e.matmul(
                            ps_sq[:], onesc_s[:], xsq[:, kp, :],
                            start=(kp == 0), stop=(kp == 7)),
                            waits=[(VP, vsq)], incs=[(TP, 1)] if kp == 7 else ())
                    tst = TP.n
                    m_row, v_row = rows[:, 0:CH], rows[:, CH:2 * CH]
                    P.emit("vector", lambda e, m_row=m_row: e.tensor_scalar(
                        out=m_row, in0=ps_sum[:], scalar1=1.0 / D, scalar2=None,
                        op0=OP.mult), waits=[(TP, tst)])
                    P.emit("vector", lambda e, m_row=m_row, v_row=v_row:
                           e.scalar_tensor_tensor(out=v_row, in0=m_row, scalar=-1.0,
                                                  in1=m_row, op0=OP.mult, op1=OP.mult))
                    P.emit("vector", lambda e, v_row=v_row: e.scalar_tensor_tensor(
                        out=v_row, in0=ps_sq[:], scalar=1.0 / D, in1=v_row,
                        op0=OP.mult, op1=OP.add), incs=[(VP, 1)])
                    mr_r = rowsr[:, 0:CH]
                    P.emit("vector", lambda e, m_row=m_row, mr_r=mr_r: e.tensor_copy(
                        out=mr_r, in_=m_row), incs=[(VP, 1)])
                    vmr = VP.n
                    rv_r = rowsr[:, CH:2 * CH]
                    P.emit("scalar", lambda e, v_row=v_row: e.activation(
                        out=v_row, in_=v_row, func=AF.Ln, bias=epsb_s[0:1, 0:1]),
                        waits=[(VP, vmr)])
                    P.emit("scalar", lambda e, v_row=v_row, rv_r=rv_r: e.activation(
                        out=rv_r, in_=v_row, func=AF.Exp, scale=-0.5), incs=[(SP, 1)])
                    P.emit("tensor", lambda e, rv_r=rv_r: e.matmul(
                        ps_rep[:], onesr_s[:], rv_r, start=True, stop=True),
                        waits=[(SP, SP.n)], incs=[(TP, 1)])
                    P.emit("scalar", lambda e: e.copy(out=rstd_sb[:], in_=ps_rep[:]),
                           waits=[(TP, TP.n), (VP, VP.n)], incs=[(SP, 1)])
                    srstd = SP.n
                    for m in range(NDP):
                        for kp in range(8):
                            P.emit("tensor", lambda e, m=m, kp=kp: e.matmul(
                                ps_proj[:, m, :], xprojT_s[:, kp, m * 128:(m + 1) * 128],
                                x_ch[:, kp, :], start=(kp == 0), stop=False),
                                waits=[(SY, xw)])
                        P.emit("tensor", lambda e, m=m: e.matmul(
                            ps_proj[:, m, :], nws_xi[:, m * 128:(m + 1) * 128],
                            mr_r, start=False, stop=True),
                            waits=[(VP, vmr)], incs=[(TP, 1)] if m == NDP - 1 else ())
                    txi = TP.n
                    if gc > 0:
                        P.emit("vector", lambda e: e.tensor_copy(
                            out=xi_ch[:, :, 0:3], in_=xi_ch[:, :, CH:CH + 3]),
                            waits=[(SP, SP.n)])
                    for m in range(NDP):
                        P.emit("vector", lambda e, m=m: e.tensor_tensor(
                            out=xi_ch[:, m, 3:3 + CH],
                            in0=ps_proj[:, m, :], in1=rstd_sb[:], op=OP.mult),
                            waits=[(TP, txi), (SP, srstd)],
                            incs=[(VP, 1)] if m == NDP - 1 else ())
                    vxi = VP.n
                    for m in range(NDP):
                        for kp in range(8):
                            P.emit("tensor", lambda e, m=m, kp=kp: e.matmul(
                                ps_proj[:, m, :], dtprojT_s[:, kp, m * 128:(m + 1) * 128],
                                x_ch[:, kp, :], start=(kp == 0), stop=False),
                                waits=[(VP, vxi)])
                        P.emit("tensor", lambda e, m=m: e.matmul(
                            ps_proj[:, m, :], nws_dt[:, m * 128:(m + 1) * 128],
                            mr_r, start=False, stop=True), incs=[(TP, 1)])
                        tdt = TP.n
                        P.emit("vector", lambda e, m=m: e.tensor_tensor(
                            out=dttmp[:], in0=ps_proj[:, m, :], in1=rstd_sb[:],
                            op=OP.mult), waits=[(TP, tdt), (SP, SP.n)],
                            incs=[(VP, 1)])
                        P.emit("scalar", lambda e, m=m: e.activation(
                            out=dttmp[:], in_=dttmp[:], func=AF.Exp,
                            bias=dtb_s[:, m:m + 1]), waits=[(VP, VP.n)])
                        P.emit("scalar", lambda e, m=m, col0=col0: e.activation(
                            out=dt_win[:, m, col0:col0 + CH], in_=dttmp[:],
                            func=AF.Ln, bias=1.0), incs=[(SP, 1)])
                    sdt = SP.n
                    for kp in range(8):
                        P.emit("tensor", lambda e, kp=kp: e.matmul(
                            ps_bc[:], bcprojT_s[:, kp, :], x_ch[:, kp, :],
                            start=(kp == 0), stop=False),
                            waits=[(SY, xw), (SP, sdt)])
                    P.emit("tensor", lambda e: e.matmul(
                        ps_bc[:], nws_bc[:], mr_r, start=False, stop=True),
                        incs=[(TP, 1)])
                    P.emit("vector", lambda e, col0=col0: e.tensor_tensor(
                        out=bc_win[:, col0:col0 + CH], in0=ps_bc[:],
                        in1=rstd_sb[0:2 * DS, :], op=OP.mult),
                        waits=[(TP, TP.n)], incs=[(VP, 1)])
                    act_table(ACT_SILU)
                    for m in range(NDP):
                        P.emit("vector", lambda e, m=m: e.tensor_scalar(
                            out=cacc[0], in0=xi_ch[:, m, 0:CH],
                            scalar1=convw_s[:, m * DC:m * DC + 1], scalar2=None,
                            op0=OP.mult), waits=[(VP, vxi)])
                        for k in range(1, DC):
                            P.emit("vector", lambda e, m=m, k=k: e.scalar_tensor_tensor(
                                out=cacc[k % 2],
                                in0=xi_ch[:, m, k:k + CH],
                                scalar=convw_s[:, m * DC + k:m * DC + k + 1],
                                in1=cacc[(k - 1) % 2], op0=OP.mult, op1=OP.add),
                                waits=[(SP, SP.n)] if k == 1 else (),
                                incs=[(VP, 1)] if k == DC - 1 else ())
                        P.emit("scalar", lambda e, m=m, col0=col0: e.activation(
                            out=xc_win[:, m, col0:col0 + CH], in_=cacc[(DC - 1) % 2],
                            func=AF.Silu, bias=convb_s[:, m:m + 1]),
                            waits=[(VP, VP.n)], incs=[(SP, 1)])
                    act_table(ACT_EXPLN)
            wxc = SP.n

            act_table(ACT_EXPLN)
            with ExitStack() as pse:
                ps_B = pse.enter_context(_psum("ps_B", [128, WIN], F32))
                ps_C = pse.enter_context(_psum("ps_C", [128, WIN], F32))
                tts_snap = [VP.n, VP.n]
                for dp in range(NDP):
                    P.emit("vector", lambda e, dp=dp: e.tensor_tensor(
                        out=dtxc[:], in0=dt_win[:, dp, :], in1=xc_win[:, dp, :],
                        op=OP.mult), waits=[(SP, wxc)])
                    P.emit("vector", lambda e, dp=dp: e.tensor_scalar(
                        out=y_win[:, dp, :], in0=xc_win[:, dp, :],
                        scalar1=dvec_s[:, dp:dp + 1], scalar2=None, op0=OP.mult),
                        waits=[(GD, GD.n)], incs=[(VP, 1)])
                    for s in range(DS):
                        sidx = dp * DS + s
                        par = s % 2
                        for q in range(4):
                            P.emit("tensor", lambda e, s=s, q=q: e.matmul(
                                ps_B[:, q * CH:(q + 1) * CH],
                                selmat_s[:, s * 128:(s + 1) * 128],
                                bc_win[:, q * CH:(q + 1) * CH],
                                start=True, stop=True),
                                waits=[(VP, VP.n), (SP, SP.n)])
                        for q in range(4):
                            P.emit("tensor", lambda e, s=s, q=q: e.matmul(
                                ps_C[:, q * CH:(q + 1) * CH],
                                selmat_s[:, (DS + s) * 128:(DS + s + 1) * 128],
                                bc_win[:, q * CH:(q + 1) * CH],
                                start=True, stop=True),
                                incs=[(TP, 1)] if q == 3 else ())
                        trep = TP.n
                        P.emit("scalar", lambda e, dp=dp, sidx=sidx, par=par:
                               e.activation(out=adisc[:, par, :], in_=dt_win[:, dp, :],
                                            func=AF.Exp, scale=a_s[:, sidx:sidx + 1]),
                               waits=[(VP, tts_snap[par])], incs=[(SP, 1)])
                        sad = SP.n
                        P.emit("vector", lambda e: e.tensor_tensor(
                            out=b_buf[:], in0=dtxc[:], in1=ps_B[:], op=OP.mult),
                            waits=[(TP, trep)])
                        P.emit("vector", lambda e, par=par, sidx=sidx:
                               e.tensor_tensor_scan(
                                   out=h_buf[:], data0=adisc[:, par, :].bitcast(F32),
                                   data1=b_buf[:],
                                   initial=hcol[:, sidx:sidx + 1],
                                   op0=OP.mult, op1=OP.add), waits=[(SP, sad)],
                               incs=[(VP, 1)])
                        tts_snap[par] = VP.n
                        P.emit("vector", lambda e: e.tensor_tensor(
                            out=b_buf[:], in0=h_buf[:], in1=ps_C[:], op=OP.mult))
                        P.emit("vector", lambda e, dp=dp: e.tensor_tensor(
                            out=y_win[:, dp, :], in0=y_win[:, dp, :], in1=b_buf[:],
                            op=OP.add), incs=[(VP, 1)])
                        P.emit("vector", lambda e, sidx=sidx: e.tensor_copy(
                            out=hcol[:, sidx:sidx + 1], in_=h_buf[:, WIN - 1:WIN]))
            vy = VP.n
            if dbg_dt is not None:
                P.dma("gpsimd", GD, ptile(dbg_dt[:, w * WIN:(w + 1) * WIN], 128),
                      dt_win[:], waits=[(VP, vy)])
                P.dma("gpsimd", GD, ptile(dbg_xc[:, w * WIN:(w + 1) * WIN], 128),
                      xc_win[:], waits=[(VP, vy)])
                P.dma("gpsimd", GD, dbg_bc[:, w * WIN:(w + 1) * WIN],
                      bc_win[:].bitcast(F32), waits=[(VP, vy)])
            P.dma("gpsimd", GD, ptile(ag_in[w].ap(), 128), y_win[:].bitcast(F32),
                  waits=[(VP, vy)])
            if dbg_y is not None:
                P.dma("gpsimd", GD, ptile(dbg_y[:, w * WIN:(w + 1) * WIN], 128),
                      y_win[:])
            P.emit("gpsimd", lambda e, w=w: e.collective_compute(
                "AllGather", OP.bypass, replica_groups=groups,
                ins=[ag_in[w].ap().opt()], outs=[ag_out[w].ap().opt()]),
                waits=[(GD, GD.n)], incs=[(CC, 1)])

        if stage == "mamba":
            P.emit("gpsimd", lambda e: e.wait_ge(CC.h, CC.n))
            with nc.Block() as blk:
                P.replay(blk)
            ab_es.close()
            return nc

        ab_es.close()
        y_own = sbuf("y_own", [128, 16, BT], BF16)
        out1 = sbuf("out1", [128, 8, BT], F32R)
        h_sb = sbuf("h_sb", [128, 8, BT], F32R)
        gu_t = sbuf("gu_t", [128, 32, BT], BF16)
        gu_sb = gu_t[:]
        scrR_t = sbuf("scrR_t", [128, 8, BT], F32R)
        scrR = scrR_t[:]
        moe = sbuf("moe", [128, 8, BT], F32)
        scrF = sbuf("scrF", [128, 8, BT], F32)
        oe_bf = y_own[:, 0:8, :]
        gtmp = sbuf("gtmp", [128, BT], F32)
        wga = [sbuf(f"wga{i}", [128, 2, 8, 128], F32R) for i in range(2)]
        wua = [sbuf(f"wua{i}", [128, 8, 128], F32R) for i in range(2)]
        wda = [y_own[:, 8 + 4 * i:12 + 4 * i, :]
               .rearrange("p a b -> p (a b)")
               .rearrange("p (f d) -> p f d", d=D) for i in range(2)]
        opa = [sbuf(f"opa{i}", [128, D], BF16) for i in range(2)]
        wia = [sbuf(f"wia{i}", [128, D], F32R) for i in range(2)]
        woa = [sbuf(f"woa{i}", [128, D], F32R) for i in range(2)]
        t1_sb = moe[:, 0:2, :].rearrange("p a b -> p (a b)")
        lg_sb = sbuf("lg_sb", [E, BT], F32R)
        lb_sb = sbuf("lb_sb", [128, E], F32)
        msk_sb = sbuf("msk_sb", [128, E], F32)
        em_sb = sbuf("em_sb", [128, E], F32)
        gt_sb = sbuf("gt_sb", [128, E], F32R)
        col_sb = sbuf("col_sb", [128, 4], F32)
        gsc_sb = sbuf("gsc_sb", [128, 8], F32)
        rrow = rows[:, 0:BT]

        slv = _STAGES.index(stage)

        def _past(name):
            return slv >= _STAGES.index(name)

        for bt in range(NB if stage == "full" else 1):
            rows_view = ag_out[bt].ap().rearrange("a (s c) -> (a s) c", s=GRP)
            for gi in range(16):
                P.emit("gpsimd", lambda e, gi=gi, rows_view=rows_view:
                       e.indirect_dma_start(
                           out=y_own[:, gi, :], out_offset=None, in_=rows_view,
                           in_offset=bass.IndirectOffsetOnAxis(
                               ap=idx_s[:, gi:gi + 1], axis=0)),
                       waits=[(CC, NW), (TP, TP.n)], incs=[(GD, 16)])
            gy = GD.n
            P.emit("vector", lambda e: e.tensor_copy(out=col_sb[:, 0:1], in_=col_sb[:, 0:1]),
                   waits=[(SP, SP.n), (TP, TP.n), (SY, SY.n), (GD, GD.n), (GP, GP.n), (CC, CC.n)],
                   incs=[(VP, 1)])
            P.emit("scalar", lambda e: e.copy(out=col_sb[:, 1:2], in_=col_sb[:, 1:2]),
                   waits=[(VP, VP.n), (TP, TP.n), (GD, GD.n)], incs=[(SP, 1)])
            P.dma("sync", SY, scrF[:],
                  ptile(xres_in[:, bt * BT:(bt + 1) * BT]),
                  waits=[(VP, VP.n), (CC, NW), (GD, GD.n)])
            xr = SY.n
            with ExitStack() as pse:
                ps_op = pse.enter_context(_psum("ps_op", [128, 8, BT], F32))
                op_last = [TP.n, TP.n]
                for kp in range(16):
                    par = kp % 2
                    P.dma("gpsimd", GD, opa[par][:],
                          outprojT[kp * 128:(kp + 1) * 128, :],
                          waits=[(TP, op_last[par]), (CC, NW)])
                    wv = GD.n
                    for m in range(8):
                        P.emit("tensor", lambda e, kp=kp, m=m, par=par: e.matmul(
                            ps_op[:, m, :], opa[par][:, m * 128:(m + 1) * 128],
                            y_own[:, kp, :], start=(kp == 0), stop=(kp == 15)),
                            waits=[(GD, wv), (GD, gy), (SP, SP.n)],
                            incs=[(TP, 1)] if m == 7 else ())
                    op_last[par] = TP.n
                top = TP.n
                for m in range(8):
                    P.emit("vector", lambda e, m=m: e.tensor_tensor(
                        out=out1[:, m, :], in0=ps_op[:, m, :], in1=scrF[:, m, :],
                        op=OP.add), waits=[(TP, top), (SY, xr)],
                        incs=[(VP, 1)] if m == 7 else ())
            vo1 = VP.n
            if dbg_o1 is not None:
                P.dma("gpsimd", GD, ptile(dbg_o1[:, bt * BT:(bt + 1) * BT]),
                      out1[:].bitcast(F32), waits=[(VP, vo1)])
            if not _past("c_h"):
                continue
            with ExitStack() as pse:
                pt = lambda n, s: pse.enter_context(_psum(n, s, F32))
                ps_s1 = pt("ps_s1", [1, BT])
                ps_s2 = pt("ps_s2", [1, BT])
                ps_rep = pt("ps_rep", [128, BT])
                P.emit("vector", lambda e: e.tensor_tensor(
                    out=scrR.rearrange("p a b -> p (a b)"),
                    in0=out1[:].rearrange("p a b -> p (a b)").bitcast(F32),
                    in1=out1[:].rearrange("p a b -> p (a b)").bitcast(F32),
                    op=OP.mult), waits=[(VP, vo1)], incs=[(VP, 1)])
                vsq = VP.n
                for kp in range(8):
                    P.emit("tensor", lambda e, kp=kp: e.matmul(
                        ps_s1[:], onesc_s[:], out1[:, kp, :],
                        start=(kp == 0), stop=(kp == 7)), waits=[(VP, vo1)])
                for kp in range(8):
                    P.emit("tensor", lambda e, kp=kp: e.matmul(
                        ps_s2[:], onesc_s[:], scrR[:, kp, :],
                        start=(kp == 0), stop=(kp == 7)),
                        waits=[(VP, vsq)], incs=[(TP, 1)] if kp == 7 else ())
                tst = TP.n
                if dbg_rows is not None and bt == 0:
                    P.emit("vector", lambda e: e.tensor_copy(
                        out=rows[:, 0:BT], in_=ps_s1[:]), waits=[(TP, tst)])
                    P.emit("vector", lambda e: e.tensor_copy(
                        out=rows[:, BT:2 * BT], in_=ps_s2[:]), incs=[(VP, 1)])
                    P.dma("gpsimd", GD, dbg_rows[0:1, 0:2 * CH], rows[:],
                          waits=[(VP, VP.n)])
                m_row, v_row = rows[:, 0:BT], rows[:, BT:2 * BT]
                P.emit("vector", lambda e, m_row=m_row: e.tensor_scalar(
                    out=m_row, in0=ps_s1[:], scalar1=1.0 / D, scalar2=None,
                    op0=OP.mult), waits=[(TP, tst), (GD, GD.n)])
                P.emit("vector", lambda e, m_row=m_row, v_row=v_row:
                       e.scalar_tensor_tensor(out=v_row, in0=m_row, scalar=-1.0,
                                              in1=m_row, op0=OP.mult, op1=OP.mult))
                P.emit("vector", lambda e, v_row=v_row: e.scalar_tensor_tensor(
                    out=v_row, in0=ps_s2[:], scalar=1.0 / D, in1=v_row,
                    op0=OP.mult, op1=OP.add), incs=[(VP, 1)])
                mr_r = rowsr[:, 0:BT]
                P.emit("vector", lambda e, m_row=m_row, mr_r=mr_r: e.tensor_copy(
                    out=mr_r, in_=m_row), incs=[(VP, 1)])
                vmr = VP.n
                rv_r = rowsr[:, BT:2 * BT]
                P.emit("scalar", lambda e, v_row=v_row: e.activation(
                    out=v_row, in_=v_row, func=AF.Ln, bias=epsb_s[0:1, 0:1]), waits=[(VP, vmr)])
                P.emit("scalar", lambda e, v_row=v_row, rv_r=rv_r: e.activation(
                    out=rv_r, in_=v_row, func=AF.Exp, scale=-0.5), incs=[(SP, 1)])
                P.emit("tensor", lambda e, rv_r=rv_r: e.matmul(
                    ps_rep[:], onesr_s[:], rv_r, start=True, stop=True),
                    waits=[(SP, SP.n)], incs=[(TP, 1)])
                P.emit("scalar", lambda e: e.copy(out=rstd_sb[:], in_=ps_rep[:]),
                       waits=[(TP, TP.n), (VP, VP.n)], incs=[(SP, 1)])
                srstd = SP.n
                if dbg_rows is not None and bt == 0:
                    P.dma("gpsimd", GD, dbg_rows[1:2, 0:2 * CH],
                          rowsr[:].bitcast(F32), waits=[(SP, srstd)])
                    P.dma("gpsimd", GD, dbg_rows[2:3, 0:2 * CH], rows[:],
                          waits=[(SP, srstd)])
                    P.dma("gpsimd", GD, dbg_rows[3:4, 0:CH], rstd_sb[:, 0:CH][0:1, :],
                          waits=[(SP, srstd)])
            if bt == 0:
                with _psum("ps_wr", [1, D], F32) as ps_wr:
                    for kp in range(8):
                        par = kp % 2
                        P.dma("sync", SY, wia[par][:],
                              winT[kp * 128:(kp + 1) * 128, :].bitcast(F32R),
                              waits=[(TP, TP.n)])
                        wv = SY.n
                        for half in range(2):
                            P.emit("tensor", lambda e, kp=kp, par=par, half=half:
                                   e.matmul(ps_wr[:, half * BT:(half + 1) * BT],
                                            onesc_s[:],
                                            wia[par][:, half * BT:(half + 1) * BT],
                                            start=(kp == 0), stop=(kp == 7)),
                                   waits=[(SY, wv), (VP, vmr), (SP, srstd)],
                                   incs=[(TP, 1)] if (kp == 7 and half == 1) else ())
                    P.emit("vector", lambda e: e.tensor_scalar(
                        out=nws_win[:], in0=ps_wr[:], scalar1=-1.0, scalar2=None,
                        op0=OP.mult), waits=[(TP, TP.n)], incs=[(VP, 1)])
            with ExitStack() as pse:
                ps_h = pse.enter_context(_psum("ps_h", [128, 8, BT], F32))
                wi_last = [TP.n, TP.n]
                for kp in range(8):
                    par = kp % 2
                    P.dma("sync", SY, wia[par][:],
                          winT[kp * 128:(kp + 1) * 128, :].bitcast(F32R),
                          waits=[(TP, wi_last[par])])
                    wv = SY.n
                    for m in range(8):
                        P.emit("tensor", lambda e, kp=kp, m=m, par=par: e.matmul(
                            ps_h[:, m, :], wia[par][:, m * 128:(m + 1) * 128],
                            out1[:, kp, :], start=(kp == 0), stop=False),
                            waits=[(SY, wv), (VP, vmr), (SP, srstd)],
                            incs=[(TP, 1)] if m == 7 else ())
                    wi_last[par] = TP.n
                for m in range(8):
                    P.emit("tensor", lambda e, m=m: e.matmul(
                        ps_h[:, m, :], nws_win[:, m * 128:(m + 1) * 128], mr_r,
                        start=False, stop=True),
                        waits=[(VP, vmr)], incs=[(TP, 1)] if m == 7 else ())
                th = TP.n
                for m in range(8):
                    P.emit("vector", lambda e, m=m: e.tensor_tensor(
                        out=scrF[:, m, :], in0=ps_h[:, m, :], in1=rstd_sb[:, 0:BT],
                        op=OP.mult), waits=[(TP, th), (SP, srstd)],
                        incs=[(VP, 1)] if m == 7 else ())
                vhx = VP.n
            if dbg_hx is not None:
                P.dma("gpsimd", GD, ptile(dbg_hx[:, bt * BT:(bt + 1) * BT]),
                      scrF[:], waits=[(VP, vhx)])
            with ExitStack() as pse:
                pt = lambda n, s: pse.enter_context(_psum(n, s, F32))
                ps_s2 = pt("ps_s2b", [1, BT])
                ps_rep = pt("ps_repb", [128, BT])
                P.emit("vector", lambda e: e.tensor_tensor(
                    out=scrR.rearrange("p a b -> p (a b)"),
                    in0=scrF[:].rearrange("p a b -> p (a b)"),
                    in1=scrF[:].rearrange("p a b -> p (a b)"),
                    op=OP.mult), waits=[(VP, vhx)], incs=[(VP, 1)])
                vsq = VP.n
                for kp in range(8):
                    P.emit("tensor", lambda e, kp=kp: e.matmul(
                        ps_s2[:], onesc_s[:], scrR[:, kp, :],
                        start=(kp == 0), stop=(kp == 7)),
                        waits=[(VP, vsq)], incs=[(TP, 1)] if kp == 7 else ())
                tst = TP.n
                v_row = rows[:, 0:BT]
                P.emit("vector", lambda e, v_row=v_row: e.tensor_scalar(
                    out=v_row, in0=ps_s2[:], scalar1=1.0 / D, scalar2=None,
                    op0=OP.mult), waits=[(TP, tst)], incs=[(VP, 1)])
                rv_r = rowsr[:, BT:2 * BT]
                P.emit("scalar", lambda e, v_row=v_row: e.activation(
                    out=v_row, in_=v_row, func=AF.Ln, bias=epsb_s[0:1, 1:2]),
                    waits=[(VP, VP.n)])
                P.emit("scalar", lambda e, v_row=v_row, rv_r=rv_r: e.activation(
                    out=rv_r, in_=v_row, func=AF.Exp, scale=-0.5), incs=[(SP, 1)])
                P.emit("tensor", lambda e, rv_r=rv_r: e.matmul(
                    ps_rep[:], onesr_s[:], rv_r, start=True, stop=True),
                    waits=[(SP, SP.n)], incs=[(TP, 1)])
                P.emit("scalar", lambda e: e.copy(out=rstd_sb[:], in_=ps_rep[:]),
                       waits=[(TP, TP.n), (VP, VP.n)], incs=[(SP, 1)])
                srstd = SP.n
                for m in range(8):
                    P.emit("vector", lambda e, m=m: e.scalar_tensor_tensor(
                        out=h_sb[:, m, :], in0=scrF[:, m, :],
                        scalar=innorm_s[:, m:m + 1], in1=rstd_sb[:, 0:BT],
                        op0=OP.mult, op1=OP.mult), waits=[(SP, srstd)],
                        incs=[(VP, 1)] if m == 7 else ())
                vh = VP.n
            if dbg_h is not None:
                P.dma("gpsimd", GD, ptile(dbg_h[:, bt * BT:(bt + 1) * BT]),
                      h_sb[:].bitcast(F32), waits=[(VP, vh)])
            if not _past("c_gate"):
                continue
            with ExitStack() as pse:
                pt = lambda n, s, d=F32: pse.enter_context(_psum(n, s, d))
                ps_l = pt("ps_l", [E, BT])
                ps_lt = pt("ps_lt", [128, E], F32R)
                ps_gt = pt("ps_gt", [E, 128], F32R)
                for kp in range(8):
                    P.emit("tensor", lambda e, kp=kp: e.matmul(
                        ps_l[:], wrT_s[:, kp * E:(kp + 1) * E], h_sb[:, kp, :],
                        start=(kp == 0), stop=(kp == 7)),
                        waits=[(VP, vh)], incs=[(TP, 1)] if kp == 7 else ())
                P.emit("vector", lambda e: e.tensor_copy(
                    out=lg_sb[:], in_=ps_l[:]), waits=[(TP, TP.n)], incs=[(VP, 1)])
                vlg = VP.n
                if dbg_lg is not None:
                    P.dma("gpsimd", GD, dbg_lg[:, bt * BT:(bt + 1) * BT],
                          lg_sb[:].bitcast(F32), waits=[(VP, vlg)])
                for tt in range(4):
                    P.emit("tensor", lambda e, tt=tt: e.transpose(
                        ps_lt[:], lg_sb[:, tt * 128:(tt + 1) * 128],
                        ident_s[0:8, 0:8]),
                        waits=[(VP, VP.n)], incs=[(TP, 1)])
                    tlt = TP.n
                    ltf = ps_lt[:].bitcast(F32)
                    P.emit("vector", lambda e, ltf=ltf: e.tensor_tensor(
                        out=lb_sb[:], in0=ltf, in1=biasrep_s[:], op=OP.add),
                        waits=[(TP, tlt)])
                    P.emit("vector", lambda e: e.drain())
                    P.emit("vector", lambda e: e.reduce_max(
                        col_sb[:, 0:1], lb_sb[:], axis=AX.X))
                    P.emit("vector", lambda e: e.drain())
                    P.emit("vector", lambda e: e.tensor_scalar(
                        out=msk_sb[:], in0=lb_sb[:], scalar1=col_sb[:, 0:1],
                        scalar2=None, op0=OP.is_ge))
                    P.emit("vector", lambda e: e.tensor_copy(
                        out=gsc_sb[:], in_=msk_sb[:]))
                    P.emit("vector", lambda e: e.scalar_tensor_tensor(
                        out=em_sb[:], in0=msk_sb[:], scalar=-BIG, in1=lb_sb[:],
                        op0=OP.mult, op1=OP.add))
                    P.emit("vector", lambda e: e.tensor_copy(
                        out=gsc_sb[:], in_=em_sb[:]))
                    P.emit("vector", lambda e: e.drain())
                    P.emit("vector", lambda e: e.reduce_max(
                        col_sb[:, 1:2], em_sb[:], axis=AX.X))
                    P.emit("vector", lambda e: e.drain())
                    P.emit("vector", lambda e: e.tensor_scalar(
                        out=msk_sb[:], in0=lb_sb[:], scalar1=col_sb[:, 1:2],
                        scalar2=None, op0=OP.is_ge))
                    P.emit("vector", lambda e: e.tensor_copy(
                        out=gsc_sb[:], in_=msk_sb[:]))
                    P.emit("vector", lambda e, ltf=ltf: e.reduce_max(
                        col_sb[:, 3:4], ltf, axis=AX.X, negate=True))
                    P.emit("vector", lambda e: e.drain(), incs=[(VP, 1)])
                    P.emit("scalar", lambda e, ltf=ltf: e.activation(
                        out=em_sb[:], in_=ltf, func=AF.Exp,
                        bias=col_sb[:, 3:4]), waits=[(VP, VP.n)], incs=[(SP, 1)])
                    P.emit("vector", lambda e: e.tensor_copy(
                        out=gsc_sb[:], in_=em_sb[:]), waits=[(SP, SP.n)])
                    P.emit("vector", lambda e: e.tensor_tensor(
                        out=em_sb[:], in0=em_sb[:], in1=msk_sb[:], op=OP.mult))
                    P.emit("vector", lambda e: e.tensor_copy(
                        out=gsc_sb[:], in_=em_sb[:]))
                    P.emit("vector", lambda e: e.drain())
                    P.emit("vector", lambda e: e.reduce_sum(
                        col_sb[:, 0:1], em_sb[:], axis=AX.X))
                    P.emit("vector", lambda e: e.drain())
                    P.emit("vector", lambda e: e.reciprocal(
                        col_sb[:, 1:2], col_sb[:, 0:1]))
                    P.emit("vector", lambda e: e.drain())
                    P.emit("vector", lambda e: e.tensor_copy(
                        out=gsc_sb[:, 0:4], in_=col_sb[:]))
                    P.emit("vector", lambda e: e.tensor_scalar(
                        out=gt_sb[:], in0=em_sb[:], scalar1=col_sb[:, 1:2],
                        scalar2=None, op0=OP.mult), incs=[(VP, 1)])
                    P.emit("tensor", lambda e: e.transpose(
                        ps_gt[:], gt_sb[:], ident_s[:]),
                        waits=[(VP, VP.n)], incs=[(TP, 1)])
                    P.emit("vector", lambda e, tt=tt: e.tensor_copy(
                        out=gates_fm[:, tt * 128:(tt + 1) * 128],
                        in_=ps_gt[:]),
                        waits=[(TP, TP.n)], incs=[(VP, 1)])
                vg = VP.n
            if dbg_g is not None:
                P.dma("gpsimd", GD, dbg_g[:, bt * BT:(bt + 1) * BT],
                      gates_fm[:].bitcast(F32), waits=[(VP, vg)])
            if not _past("c_exp"):
                continue
            act_table(ACT_SILU)
            for ex in range(E if _past("c_epi") else 1):
                with ExitStack() as pse:
                    ps_g = [pse.enter_context(_psum(f"ps_g{i}", [128, BT], F32))
                            for i in range(2)]
                    ps_u = [pse.enter_context(_psum(f"ps_u{i}", [128, BT], F32))
                            for i in range(2)]
                    mm_last = [TP.n, TP.n]
                    u_last = [TP.n, TP.n]
                    gu_last = [VP.n, VP.n]
                    for fb2 in range(16):
                        par = fb2 % 2
                        P.dma("sync", SY, wga[par][:],
                              wg_t[ex, 2 * fb2:2 * fb2 + 2]
                              .rearrange("f (a p) b -> p f a b", p=128).bitcast(F32R),
                              waits=[(TP, mm_last[par])])
                        wv0 = SY.n
                        for half in range(2):
                            fb = 2 * fb2 + half
                            P.dma("sync", SY, wua[half][:],
                                  ptile(wu_t[ex, fb].bitcast(F32R)),
                                  waits=[(TP, u_last[half])])
                            wvu = SY.n
                            for kp in range(8):
                                P.emit("tensor", lambda e, kp=kp, par=par, half=half:
                                       e.matmul(ps_g[par][:],
                                                wga[par][:, half, kp, :],
                                                h_sb[:, kp, :],
                                                start=(kp == 0), stop=(kp == 7)),
                                       waits=[(SY, wv0),
                                              (VP, gu_last[par])])
                            for kp in range(8):
                                P.emit("tensor", lambda e, kp=kp, par=par, half=half:
                                       e.matmul(ps_u[par][:],
                                                wua[half][:, kp, :],
                                                h_sb[:, kp, :],
                                                start=(kp == 0), stop=(kp == 7)),
                                       waits=[(SY, wvu)],
                                       incs=[(TP, 1)] if kp == 7 else ())
                            u_last[half] = TP.n
                            tgu = TP.n
                            P.emit("scalar", lambda e, par=par: e.activation(
                                out=gtmp[:], in_=ps_g[par][:], func=AF.Silu),
                                waits=[(TP, tgu), (VP, gu_last[1 - par])],
                                incs=[(SP, 1)])
                            P.emit("vector", lambda e, fb=fb, par=par: e.tensor_tensor(
                                out=gu_sb[:, fb, :], in0=ps_u[par][:], in1=gtmp[:],
                                op=OP.mult), waits=[(SP, SP.n)], incs=[(VP, 1)])
                            gu_last[par] = VP.n
                        mm_last[par] = TP.n
                vgu = VP.n
                if not _past("c_wd"):
                    continue
                with ExitStack() as pse:
                    ps_oe = pse.enter_context(_psum("ps_oe", [128, 8, BT], F32))
                    wd_last = [TP.n, TP.n]
                    for kf2 in range(16):
                        par = kf2 % 2
                        P.dma("gpsimd", GD, wda[par],
                              wd_t[ex, 2 * kf2:2 * kf2 + 2]
                              .rearrange("f p d -> p f d"),
                              waits=[(TP, wd_last[par])])
                        wv = GD.n
                        for half in range(2):
                            kfb = 2 * kf2 + half
                            for m in range(8):
                                P.emit("tensor", lambda e, kfb=kfb, m=m, par=par,
                                       half=half: e.matmul(
                                           ps_oe[:, m, :],
                                           wda[par][:, half, m * 128:(m + 1) * 128],
                                           gu_sb[:, kfb, :],
                                           start=(kfb == 0), stop=(kfb == 31)),
                                       waits=[(GD, wv), (VP, vgu)],
                                       incs=[(TP, 1)] if m == 7 else ())
                        wd_last[par] = TP.n
                    toe = TP.n
                    for m in range(8):
                        P.emit("vector", lambda e, m=m: e.tensor_copy(
                            out=oe_bf[:, m, :], in_=ps_oe[:, m, :]),
                            waits=[(TP, toe)], incs=[(VP, 1)] if m == 7 else ())
                    voe = VP.n
                if not _past("c_epi"):
                    continue
                with ExitStack() as pse:
                    pt = lambda n, s: pse.enter_context(_psum(n, s, F32))
                    ps_s2 = pt("ps_s2e", [1, BT])
                    ps_gr = pt("ps_gr", [128, BT])
                    ps_gx = pt("ps_gx", [1, BT])
                    P.emit("vector", lambda e: e.tensor_tensor(
                        out=scrR.rearrange("p a b -> p (a b)"),
                        in0=oe_bf.rearrange("p a b -> p (a b)"),
                        in1=oe_bf.rearrange("p a b -> p (a b)"),
                        op=OP.mult), waits=[(VP, voe)], incs=[(VP, 1)])
                    vsq = VP.n
                    for kp in range(8):
                        P.emit("tensor", lambda e, kp=kp: e.matmul(
                            ps_s2[:], onesc_s[:], scrR[:, kp, :],
                            start=(kp == 0), stop=(kp == 7)),
                            waits=[(VP, vsq)], incs=[(TP, 1)] if kp == 7 else ())
                    tst = TP.n
                    P.emit("vector", lambda e: e.tensor_scalar(
                        out=rrow, in0=ps_s2[:], scalar1=1.0 / D, scalar2=EPS_RMS,
                        op0=OP.mult, op1=OP.add), waits=[(TP, tst)], incs=[(VP, 1)])
                    P.emit("gpsimd", lambda e: e.tensor_tensor(
                        out=rrow, in0=rrow, in1=powm[:], op=OP.pow),
                        waits=[(VP, VP.n)], incs=[(GP, 1)])
                    P.emit("tensor", lambda e, ex=ex: e.matmul(
                        ps_gx[:], ident_s[0:8, ex:ex + 1], gates_fm[0:8, :],
                        start=True, stop=True),
                        waits=[(VP, vg)], incs=[(TP, 1)])
                    P.emit("vector", lambda e, ex=ex: e.tensor_tensor(
                        out=rowsr[:, 0:BT], in0=rrow, in1=ps_gx[:],
                        op=OP.mult), waits=[(GP, GP.n), (TP, TP.n)], incs=[(VP, 1)])
                    P.emit("tensor", lambda e: e.matmul(
                        ps_gr[:], onesr_s[:], rowsr[:, 0:BT], start=True, stop=True),
                        waits=[(VP, VP.n)], incs=[(TP, 1)])
                    tgr = TP.n
                    for m in range(8):
                        P.emit("vector", lambda e, ex=ex, m=m: e.scalar_tensor_tensor(
                            out=scrF[:, m, :], in0=oe_bf[:, m, :],
                            scalar=enorm_s[:, ex * 8 + m:ex * 8 + m + 1],
                            in1=ps_gr[:], op0=OP.mult, op1=OP.mult),
                            waits=[(TP, tgr)])
                        if ex == 0:
                            P.emit("vector", lambda e, m=m: e.tensor_copy(
                                out=moe[:, m, :], in_=scrF[:, m, :]),
                                incs=[(VP, 1)] if m == 7 else ())
                        else:
                            P.emit("vector", lambda e, m=m: e.tensor_tensor(
                                out=moe[:, m, :], in0=moe[:, m, :], in1=scrF[:, m, :],
                                op=OP.add), incs=[(VP, 1)] if m == 7 else ())
            vmoe = VP.n
            if not _past("c_out"):
                continue
            act_table(ACT_EXPLN)
            with ExitStack() as pse:
                pt = lambda n, s: pse.enter_context(_psum(n, s, F32))
                ps_s2 = pt("ps_s2o", [1, BT])
                ps_rep = pt("ps_repo", [128, BT])
                P.emit("vector", lambda e: e.tensor_tensor(
                    out=scrR.rearrange("p a b -> p (a b)"),
                    in0=moe[:].rearrange("p a b -> p (a b)"),
                    in1=moe[:].rearrange("p a b -> p (a b)"),
                    op=OP.mult), waits=[(VP, vmoe)], incs=[(VP, 1)])
                vsq = VP.n
                for kp in range(8):
                    P.emit("tensor", lambda e, kp=kp: e.matmul(
                        ps_s2[:], onesc_s[:], scrR[:, kp, :],
                        start=(kp == 0), stop=(kp == 7)),
                        waits=[(VP, vsq)], incs=[(TP, 1)] if kp == 7 else ())
                tst = TP.n
                v_row = rows[:, 0:BT]
                P.emit("vector", lambda e, v_row=v_row: e.tensor_scalar(
                    out=v_row, in0=ps_s2[:], scalar1=1.0 / D, scalar2=None,
                    op0=OP.mult), waits=[(TP, tst)], incs=[(VP, 1)])
                rv_r = rowsr[:, BT:2 * BT]
                P.emit("scalar", lambda e, v_row=v_row: e.activation(
                    out=v_row, in_=v_row, func=AF.Ln, bias=epsb_s[0:1, 1:2]),
                    waits=[(VP, VP.n)])
                P.emit("scalar", lambda e, v_row=v_row, rv_r=rv_r: e.activation(
                    out=rv_r, in_=v_row, func=AF.Exp, scale=-0.5), incs=[(SP, 1)])
                P.emit("tensor", lambda e, rv_r=rv_r: e.matmul(
                    ps_rep[:], onesr_s[:], rv_r, start=True, stop=True),
                    waits=[(SP, SP.n)], incs=[(TP, 1)])
                P.emit("scalar", lambda e: e.copy(out=rstd_sb[:], in_=ps_rep[:]),
                       waits=[(TP, TP.n), (VP, VP.n)], incs=[(SP, 1)])
                srstd = SP.n
                for m in range(8):
                    P.emit("vector", lambda e, m=m: e.scalar_tensor_tensor(
                        out=scrR[:, m, :], in0=moe[:, m, :],
                        scalar=outnorm_s[:, m:m + 1], in1=rstd_sb[:, 0:BT],
                        op0=OP.mult, op1=OP.mult), waits=[(SP, srstd)],
                        incs=[(VP, 1)] if m == 7 else ())
                vnrm = VP.n
            with ExitStack() as pse:
                ps_f = pse.enter_context(_psum("ps_f", [128, 4, 2, BT], F32))
                wo_last = [TP.n, TP.n]
                for kp in range(8):
                    par = kp % 2
                    P.dma("sync", SY, woa[par][:],
                          woutT[kp * 128:(kp + 1) * 128, :].bitcast(F32R),
                          waits=[(TP, wo_last[par])])
                    wv = SY.n
                    for tt in range(4):
                        for no2 in range(2):
                            P.emit("tensor", lambda e, kp=kp, tt=tt, no2=no2, par=par:
                                   e.matmul(ps_f[:, tt, no2, :],
                                            scrR[:, kp, tt * 128:(tt + 1) * 128],
                                            woa[par][:, no2 * BT:(no2 + 1) * BT],
                                            start=(kp == 0), stop=(kp == 7)),
                                   waits=[(SY, wv), (VP, vnrm)],
                                   incs=[(TP, 1)] if (tt == 3 and no2 == 1)
                                   else ())
                    wo_last[par] = TP.n
                tf = TP.n
                for tt in range(4):
                    for no2 in range(2):
                        P.emit("vector", lambda e, tt=tt, no2=no2: e.tensor_copy(
                            out=scrF[:, 2 * tt + no2, :], in_=ps_f[:, tt, no2, :]),
                            waits=[(TP, tf)],
                            incs=[(VP, 1)] if (tt == 3 and no2 == 1) else ())
                vf = VP.n
            with ExitStack() as pse:
                ps_t1 = pse.enter_context(_psum("ps_t1", [128, 8, 128], F32R))
                for tt in range(4):
                    for m in range(8):
                        P.emit("tensor", lambda e, tt=tt, m=m: e.transpose(
                            ps_t1[:, m, :], out1[:, m, tt * 128:(tt + 1) * 128],
                            ident_s[:]),
                            waits=[(VP, VP.n), (SP, SP.n)],
                            incs=[(TP, 1)] if m == 7 else ())
                    tt1 = TP.n
                    P.emit("scalar", lambda e: e.copy(
                        out=t1_sb, in_=ps_t1[:].rearrange("p a b -> p (a b)")
                        .bitcast(F32)), waits=[(TP, tt1)], incs=[(SP, 1)])
                    st1 = SP.n
                    for no2 in range(2):
                        P.emit("vector", lambda e, tt=tt, no2=no2: e.tensor_tensor(
                            out=scrF[:, 2 * tt + no2, :],
                            in0=scrF[:, 2 * tt + no2, :],
                            in1=t1_sb[:, no2 * BT:(no2 + 1) * BT], op=OP.add),
                            waits=[(SP, st1)],
                            incs=[(VP, 1)] if no2 == 1 else ())
                    vfin = VP.n
                    P.dma("gpsimd", GD,
                          out_hbm[bt * BT + tt * 128: bt * BT + (tt + 1) * 128, :],
                          scrF[:, 2 * tt:2 * tt + 2, :]
                          .rearrange("p a b -> p (a b)"), waits=[(VP, vfin)])
        P.emit("gpsimd", lambda e: e.wait_ge(GD.h, GD.n))
        with nc.Block() as blk:
            P.replay(blk)
        return nc


def _prep_core(inp, c):
    b, r = c // 4, c % 4
    sl = slice(r * DIS, (r + 1) * DIS)
    f32 = lambda a: np.ascontiguousarray(a, dtype=np.float32)
    x = inp["L0_attn_x"][b]
    x_fm = f32(x.T)
    own_cols = np.concatenate([np.arange(512 * (4 * w + r), 512 * (4 * w + r) + BT)
                               for w in range(NW)])
    xres = f32(x_fm[:, own_cols])
    pt = lambda a, n: f32(a).reshape(NDP, 128, n).transpose(1, 0, 2).reshape(128, NDP * n)
    d = {
        "x_fm": x_fm,
        "xres": xres,
        "xprojT": f32(inp["L0_attn_x_proj_w"][sl].T),
        "dtprojT": f32(inp["L0_attn_dt_proj_w"][sl].T),
        "bcprojT": f32(np.concatenate([inp["L0_attn_B_proj_w"],
                                       inp["L0_attn_C_proj_w"]], 0).T),
        "dtb": pt(inp["L0_attn_dt_proj_b"][sl], 1),
        "alog": pt(inp["L0_attn_A_log"][sl], DS),
        "convw": pt(inp["L0_attn_conv_w"][sl, 0, :], DC),
        "convb": pt(inp["L0_attn_conv_b"][sl], 1),
        "dvec": pt(inp["L0_attn_D"][sl], 1),
        "outprojT": f32(inp["L0_attn_out_proj_w"].T),
        "winT": f32(inp["L0_ffn_W_in"].T),
        "innorm": f32(inp["L0_ffn_in_norm_w"]).reshape(8, 128).T.copy(),
        "wrT": f32(inp["L0_ffn_Wr"].T).reshape(8, 128, E).transpose(1, 0, 2)
                .reshape(128, 8 * E).copy(),
        "biasrep": np.broadcast_to(f32(inp["L0_ffn_expert_bias"]), (128, E)).copy(),
        "wg_t": f32(inp["L0_ffn_Wg"]).transpose(0, 2, 1)
                .reshape(E, D, 32, 128).transpose(0, 2, 1, 3).copy(),
        "wu_t": f32(inp["L0_ffn_Wu"]).transpose(0, 2, 1)
                .reshape(E, D, 32, 128).transpose(0, 2, 1, 3).copy(),
        "wd_t": f32(inp["L0_ffn_Wd"]).transpose(0, 2, 1)
                .reshape(E, 32, 128, D).copy(),
        "enorm": f32(inp["L0_ffn_enorm_w"]).reshape(E, 8, 128)
                .transpose(2, 0, 1).reshape(128, E * 8).copy(),
        "outnorm": f32(inp["L0_ffn_out_norm_w"]).reshape(8, 128).T.copy(),
        "woutT": f32(inp["L0_ffn_W_out"].T),
        "ident": np.eye(128, dtype=np.float32),
        "onesc": np.ones((128, 1), np.float32),
        "onesr": np.ones((1, 128), np.float32),
        "idx": (np.arange(128)[:, None] * GRP + np.arange(16)[None, :] * 128 * GRP
                + r).astype(np.uint32),
        "selmat": np.repeat(np.eye(2 * DS, dtype=np.float32), 128, axis=1),
        "epsb": np.stack([np.full(128, EPS_LN, np.float32),
                          np.full(128, EPS_RMS, np.float32)], 1),
    }
    return d


def kernel(**inputs):
    in_maps = [_prep_core(inputs, c) for c in range(NCORE)]
    nc = build_program("full")
    res = run_bass_kernel_spmd(nc, in_maps, core_ids=list(range(NCORE)))
    out = np.zeros((B, T, D), dtype=np.float32)
    for c in range(NCORE):
        b, r = c // 4, c % 4
        oc = res.results[c]["out"]
        for w in range(NW):
            t0 = 512 * (4 * w + r)
            out[b, t0:t0 + BT, :] = oc[w * BT:(w + 1) * BT, :]
    return out
